# version 07
# baseline (speedup 1.0000x reference)
import contextlib
import numpy as np
import ml_dtypes
import concourse.bass as bass
import concourse.mybir as mybir
from concourse.bass_utils import run_bass_kernel_spmd

F32 = mybir.dt.float32
BF16 = mybir.dt.bfloat16
AF = mybir.ActivationFunctionType
ALU = mybir.AluOpType
AX = mybir.AxisListType.X
BF = ml_dtypes.bfloat16

T, D, H, HD = 8192, 4096, 4, 1024
C = 32
NG1, NG2 = 16, 8
GK = 4
NCORE = 8
EPS = 1e-5
Q0, Q1 = 611, 8191
NVALID = Q1 - Q0
SCALING = 2.0


def colperm(c):
    return (c % 8) * 4 + (c // 8)


def invperm(cc):
    return 8 * (cc % 4) + cc // 4


class Sem:
    def __init__(self, es, nc, name):
        self.h = es.enter_context(nc.semaphore(name))
        self.n = 0


class Phase:

    def __init__(self, nc):
        self.nc = nc
        self.q = {k: [] for k in "tvsgy"}

    def op(self, k, method, *args, wait=(), inc=None, incn=1, **kwargs):
        tgt = None
        if inc is not None:
            inc.n += incn
            tgt = inc.n
        waits = [(s.h, v) for (s, v) in wait]

        def f(eng, method=method, args=args, kwargs=kwargs, waits=waits,
              inc=inc, incn=incn):
            for h, v in waits:
                eng.wait_ge(h, v)
            ins = getattr(eng, method)(*args, **kwargs)
            if inc is not None:
                ins.then_inc(inc.h, incn)
        self.q[k].append(f)
        return tgt

    def wait(self, k, sem, val):
        self.q[k].append(lambda eng, h=sem.h, v=val: eng.wait_ge(h, v))

    def run(self):
        with self.nc.Block() as block:
            for key, starter in (("g", block.gpsimd), ("y", block.sync),
                                 ("t", block.tensor), ("s", block.scalar),
                                 ("v", block.vector)):
                fns = self.q[key]
                if fns:
                    starter(lambda eng, fns=fns: [f(eng) for f in fns])


def build_program():
    nc = bass.Bass("TRN2", target_bir_lowering=False, num_devices=NCORE)
    AP = bass.AP

    x = nc.dram_tensor("x", [T, D], F32, kind="ExternalInput")
    xt = nc.dram_tensor("xt", [D, T], F32, kind="ExternalInput")
    qsrc = nc.dram_tensor("qsrc", [2, 128, C, 512], F32, kind="ExternalInput")
    xr = nc.dram_tensor("xr", [2, 512, D], F32, kind="ExternalInput")
    msk = nc.dram_tensor("msk", [NG1 + NG2, 128, GK, 512], BF16, kind="ExternalInput")
    mq = nc.dram_tensor("mq", [128, 8], BF16, kind="ExternalInput")
    wt = nc.dram_tensor("wt", [D, D], F32, kind="ExternalInput")
    ats = nc.dram_tensor("ats", [128, C, 128], F32, kind="ExternalInput")
    bm = nc.dram_tensor("bm", [128, D], F32, kind="ExternalInput")
    rwts = nc.dram_tensor("rwts", [128, C, 8], F32, kind="ExternalInput")
    rbs = nc.dram_tensor("rbs", [1, 8], F32, kind="ExternalInput")
    biasr = nc.dram_tensor("biasr", [1, D], F32, kind="ExternalInput")
    bvq = nc.dram_tensor("bvq", [4, HD], F32, kind="ExternalInput")
    bvk = nc.dram_tensor("bvk", [4, HD], F32, kind="ExternalInput")
    bvv = nc.dram_tensor("bvv", [4, HD], F32, kind="ExternalInput")
    identb = nc.dram_tensor("identb", [128, 128], BF16, kind="ExternalInput")
    identf = nc.dram_tensor("identf", [128, 128], F32, kind="ExternalInput")
    ones1f = nc.dram_tensor("ones1f", [1, 128], F32, kind="ExternalInput")
    ones1b = nc.dram_tensor("ones1b", [1, 128], BF16, kind="ExternalInput")
    rep8 = nc.dram_tensor("rep8", [8, 128], F32, kind="ExternalInput")
    epsc = nc.dram_tensor("epsc", [128, 1], F32, kind="ExternalInput")

    out = nc.dram_tensor("out", [1024, D], F32, kind="ExternalOutput")

    y1sp = nc.dram_tensor("y1sp", [8, 128, D], F32, kind="Internal")
    y1td = nc.dram_tensor("y1td", [C, 128, 1024], BF16, kind="Internal")
    spd = nc.dram_tensor("spd", [8, 128, D], F32, kind="Internal")
    qsb_d = nc.dram_tensor("qsb_d", [1, D], F32, kind="Internal")
    qsa_d = nc.dram_tensor("qsa_d", [1, D], F32, kind="Internal")
    vvrow_d = nc.dram_tensor("vvrow_d", [C, 128], F32, kind="Internal")
    ccin = nc.dram_tensor("ccin", [1, D], F32, kind="Internal")
    ccout = nc.dram_tensor("ccout", [1, D], F32, kind="Internal")

    ES = contextlib.ExitStack()
    with ES:
        sbp = lambda name, shape, dt: ES.enter_context(nc.sbuf_tensor(name, shape, dt))
        identb_s = sbp("identb_s", [128, 128], BF16)
        identf_s = sbp("identf_s", [128, 128], F32)
        ones1f_s = sbp("ones1f_s", [1, 128], F32)
        ones1b_s = sbp("ones1b_s", [1, 128], BF16)
        rep8_s = sbp("rep8_s", [8, 128], F32)
        mq_s = sbp("mq_s", [128, 8], BF16)
        qkv32 = sbp("qkv32", [128, C], F32)
        stat1 = sbp("stat1", [128, 32], F32)
        stat2 = sbp("stat2", [128, 32], F32)
        svar = sbp("svar", [128, 2], F32)
        rout128 = sbp("rout128", [128, 1], F32)
        rbs_s = sbp("rbs_s", [1, 8], F32)
        epsc_s = sbp("epsc_s", [128, 1], F32)

        sK = Sem(ES, nc, "sK")
        sV = Sem(ES, nc, "sVd")
        sG = Sem(ES, nc, "sG")
        sY = Sem(ES, nc, "sY")
        sT = Sem(ES, nc, "sT")
        sS = Sem(ES, nc, "sS")
        sE = Sem(ES, nc, "sE")
        sCC = Sem(ES, nc, "sCC")
        sM = Sem(ES, nc, "sM")
        sX = Sem(ES, nc, "sX")
        sSp = Sem(ES, nc, "sSp")
        sTs = Sem(ES, nc, "sTs")
        sQt = Sem(ES, nc, "sQt")
        sW0 = Sem(ES, nc, "sW0")
        sW1 = Sem(ES, nc, "sW1")
        sSd = Sem(ES, nc, "sSd")
        sR0 = Sem(ES, nc, "sR0")
        sR1 = Sem(ES, nc, "sR1")
        sO = Sem(ES, nc, "sO")

        def vop(ph, method, *args, wait=(), **kw):
            v = ph.op("v", method, *args, wait=wait, inc=sE, **kw)
            ph.wait("v", sE, v)
            return v

        def tpose(ph, out_ap, in_ap, id_ap, wait=()):
            v = ph.op("t", "transpose", out_ap, in_ap, id_ap, wait=wait, inc=sT)
            ph.wait("t", sT, v)
            return v

        ph = Phase(nc)
        for dst, src in ((identb_s, identb), (identf_s, identf), (ones1f_s, ones1f),
                         (ones1b_s, ones1b), (rep8_s, rep8), (mq_s, mq), (rbs_s, rbs),
                         (epsc_s, epsc)):
            ph.op("y", "dma_start", dst[:, :], src.ap(), inc=sY, incn=16)
        ph.wait("y", sY, sY.n)
        ph.run()

        ph = Phase(nc)
        with contextlib.ExitStack() as E1:
            sb1 = lambda name, shape, dt: E1.enter_context(nc.sbuf_tensor(name, shape, dt))
            bv = sb1("bv", [4, 3 * HD], F32)
            hard = sb1("hard", [4, 3 * HD], F32)
            alpha = sb1("alpha", [4, 3], F32)
            tscr = sb1("tscr", [4, HD], F32)
            qkvhm = sb1("qkvhm", [4, HD], F32)
            vv32 = sb1("vv32", [128, C], F32)
            vvT32 = sb1("vvT32", [C, 128], F32)
            ps_b = E1.enter_context(nc.psum_tensor("ps_b", [128, C], F32))
            ps_c = E1.enter_context(nc.psum_tensor("ps_c", [C, 128], F32))

            for i, src in enumerate((bvq, bvk, bvv)):
                ph.op("y", "dma_start", bv[:, i * HD:(i + 1) * HD], src.ap(),
                      inc=sY, incn=16)
            wld = sY.n
            for i in range(3):
                sl = bv[:, i * HD:(i + 1) * HD]
                hsl = hard[:, i * HD:(i + 1) * HD]
                vop(ph, "reduce_sum", alpha[:, i:i + 1], sl, AX,
                    apply_absolute_value=True, wait=[(sY, wld)])
                vop(ph, "tensor_scalar", alpha[:, i:i + 1], alpha[:, i:i + 1],
                    1.0 / HD, None, ALU.mult)
                wsg = ph.op("s", "activation", hsl, sl, AF.Sign,
                            wait=[(sY, wld)], inc=sS)
                vop(ph, "tensor_tensor", tscr[:, :], hsl, hsl, ALU.mult,
                    wait=[(sS, wsg)])
                vop(ph, "tensor_scalar", tscr[:, :], tscr[:, :], -1.0, 1.0,
                    ALU.mult, ALU.add)
                vop(ph, "tensor_tensor", hsl, hsl, tscr[:, :], ALU.add)
                vop(ph, "tensor_scalar", hsl, hsl, alpha[:, i:i + 1], None, ALU.mult)
            wqk = vop(ph, "tensor_tensor", qkvhm[:, :], hard[:, 0:HD],
                      hard[:, HD:2 * HD], ALU.mult)
            for j in range(8):
                w1 = tpose(ph, ps_b[:, 4 * j:4 * j + 4],
                           qkvhm[:, j * 128:(j + 1) * 128], identf_s[0:4, 0:4],
                           wait=[(sE, wqk)] if j == 0 else ())
            wcp1 = ph.op("v", "tensor_copy", qkv32[:, :], ps_b[:, :],
                         wait=[(sT, w1)], inc=sE)
            for j in range(8):
                w2 = tpose(ph, ps_b[:, 4 * j:4 * j + 4],
                           hard[:, 2 * HD + j * 128:2 * HD + (j + 1) * 128],
                           identf_s[0:4, 0:4],
                           wait=[(sE, wcp1)] if j == 0 else ())
            wcp2 = ph.op("v", "tensor_copy", vv32[:, :], ps_b[:, :],
                         wait=[(sT, w2)], inc=sE)
            w3 = tpose(ph, ps_c[:, :], vv32[:, :], identf_s[:, :],
                       wait=[(sE, wcp2)])
            wvt = ph.op("v", "tensor_copy", vvT32[:, :], ps_c[:, :],
                        wait=[(sT, w3)], inc=sE)
            ph.op("y", "dma_start", vvrow_d.ap(), vvT32[:, :],
                  wait=[(sE, wvt)], inc=sY, incn=16)
            ph.wait("y", sY, sY.n)
            ph.run()

        with contextlib.ExitStack() as EA:
            acc = EA.enter_context(nc.sbuf_tensor("acc", [128, 4 * D], F32))

            def sweep(sweep_i, NG):
                ph = Phase(nc)
                with contextlib.ExitStack() as E2:
                    sb2 = lambda n, sh, dt: E2.enter_context(
                        nc.sbuf_tensor(f"{n}_{sweep_i}", sh, dt))
                    Qs = sb2("Qs", [128, C * 512], BF16)
                    wqs = None
                    with nc.sbuf_tensor(f"Qraw_{sweep_i}", [128, C * 512], F32) as Qraw:
                        wqr = ph.op("g", "dma_start",
                                    AP(Qraw, 0, [[C * 512, 128], [1, C * 512]]),
                                    AP(qsrc, sweep_i * 128 * C * 512,
                                       [[C * 512, 128], [1, C * 512]]),
                                    inc=sG, incn=16)
                        for c in range(C):
                            wqs = ph.op("v", "tensor_scalar",
                                        Qs[:, c * 512:(c + 1) * 512],
                                        Qraw[:, c * 512:(c + 1) * 512],
                                        qkv32[:, colperm(c):colperm(c) + 1], None,
                                        ALU.mult,
                                        wait=[(sG, wqr)] if c == 0 else (), inc=sE)
                    Kg = sb2("Kg", [128, C * 512], BF16)
                    Vg = sb2("Vg", [128, GK * D], BF16)
                    Pt = sb2("Pt", [128, 16 * 512], BF16)
                    Mg = sb2("Mg", [128, GK * 512], BF16)
                    ps_sc = [E2.enter_context(
                        nc.psum_tensor(f"ps_sc{sweep_i}_{i}", [128, 512], F32))
                             for i in range(2)]
                    ps_o = [E2.enter_context(
                        nc.psum_tensor(f"ps_o{sweep_i}_{i}", [128, 512], F32))
                            for i in range(4)]

                    wmz = ph.op("v", "memset", acc[:, :], 0.0, inc=sE)
                    ph.wait("v", sE, wmz)

                    goff = 0 if sweep_i == 0 else NG1
                    kdone, vdone, mdone = {}, {}, {}
                    sigval, mm1val, maskval, chval, addval = {}, {}, {}, {}, {}
                    ph1done, ph2done = {}, {}

                    def emit_dma(g):
                        w = [(sE, wqs)] if g == 0 else [(sT, ph1done[g - 1])]
                        for cs in range(8):
                            ph.op("g", "dma_start",
                                  AP(Kg, cs * 4 * 512,
                                     [[C * 512, 128], [512, 4], [1, 512]]),
                                  AP(xt, (cs * 4 * 128) * T + 512 * g,
                                     [[T, 128], [128 * T, 4], [1, 512]]),
                                  wait=w if cs == 0 else (), inc=sK, incn=16)
                        kdone[g] = sK.n
                        wv = [(sE, wqs)] if g == 0 else [(sT, ph2done[g - 1])]
                        ph.op("g", "dma_start",
                              AP(Vg, 0, [[GK * D, 128], [D, GK], [1, D]]),
                              AP(x, 512 * g * D, [[D, 128], [128 * D, GK], [1, D]]),
                              wait=wv, inc=sV, incn=16)
                        vdone[g] = sV.n
                        wm = [] if g == 0 else [(sE, maskval[(g - 1, 15)])]
                        ph.op("y", "dma_start",
                              AP(Mg, 0, [[GK * 512, 128], [1, GK * 512]]),
                              AP(msk, (goff + g) * 128 * GK * 512,
                                 [[GK * 512, 128], [1, GK * 512]]),
                              wait=wm, inc=sM, incn=16)
                        mdone[g] = sM.n

                    emit_dma(0)
                    ci_glob = 0
                    for g in range(NG):
                        for i in range(16):
                            j, h = i // 4, i % 4
                            gi = g * 16 + i
                            bank = ps_sc[gi % 2]
                            for cc in range(8):
                                c = 8 * h + cc
                                w = []
                                if cc == 0:
                                    if i == 0:
                                        w.append((sK, kdone[g]))
                                        if g == 0:
                                            w.append((sE, wqs))
                                    if gi >= 2:
                                        w.append((sS, sigval[gi - 2]))
                                ph.op("t", "matmul", bank[:, :],
                                      Kg[:, c * 512 + j * 128:c * 512 + j * 128 + 128],
                                      Qs[:, c * 512:(c + 1) * 512],
                                      start=(cc == 0), stop=(cc == 7),
                                      wait=w, inc=sT if cc == 7 else None)
                            mm1val[gi] = sT.n
                            w = [(sT, mm1val[gi])]
                            if i == 0 and g > 0:
                                w.append((sT, ph2done[g - 1]))
                            sigval[gi] = ph.op("s", "activation",
                                               Pt[:, i * 512:(i + 1) * 512],
                                               bank[:, :], AF.Sigmoid,
                                               scale=0.125, wait=w, inc=sS)
                            w = [(sS, sigval[gi])]
                            if i == 0:
                                w.append((sM, mdone[g]))
                            maskval[(g, i)] = ph.op(
                                "v", "tensor_tensor", Pt[:, i * 512:(i + 1) * 512],
                                Pt[:, i * 512:(i + 1) * 512],
                                Mg[:, j * 512:(j + 1) * 512], ALU.mult,
                                wait=w, inc=sE)
                        ph1done[g] = sT.n
                        first_chain = True
                        for h in range(H):
                            for qs in range(4):
                                for dh in range(2):
                                    bank = ps_o[ci_glob % 4]
                                    for j in range(GK):
                                        w = []
                                        if j == 0:
                                            if first_chain:
                                                w.append((sV, vdone[g]))
                                                w.append((sE, maskval[(g, 15)]))
                                            if ci_glob >= 4:
                                                w.append((sE, addval[ci_glob - 4]))
                                        ph.op("t", "matmul", bank[:, :],
                                              Pt[:, (4 * j + h) * 512 + qs * 128:
                                                 (4 * j + h) * 512 + qs * 128 + 128],
                                              Vg[:, j * D + h * HD + dh * 512:
                                                 j * D + h * HD + dh * 512 + 512],
                                              start=(j == 0), stop=(j == GK - 1),
                                              wait=w, inc=sT if j == GK - 1 else None)
                                    chval[ci_glob] = sT.n
                                    first_chain = False
                                    wadd = [(sT, chval[ci_glob])]
                                    if ci_glob >= 32:
                                        wadd.append((sE, addval[ci_glob - 32]))
                                    addval[ci_glob] = ph.op(
                                        "v", "tensor_tensor",
                                        acc[:, qs * D + h * HD + dh * 512:
                                            qs * D + h * HD + dh * 512 + 512],
                                        acc[:, qs * D + h * HD + dh * 512:
                                            qs * D + h * HD + dh * 512 + 512],
                                        bank[:, :], ALU.add,
                                        wait=wadd, inc=sE)
                                    ci_glob += 1
                        ph2done[g] = sT.n
                        if g + 1 < NG:
                            emit_dma(g + 1)
                    ph.run()

            def ln1(p):
                ph = Phase(nc)
                with contextlib.ExitStack() as E3:
                    sb3 = lambda n, sh, dt: E3.enter_context(
                        nc.sbuf_tensor(f"{n}_{p}", sh, dt))
                    vvrow = sb3("vvrow", [1, D], F32)
                    vvb = sb3("vvb", [128, D], F32)
                    xrb = sb3("xrb", [128, D], F32)
                    s1 = sb3("s1", [128, D], F32)
                    y1f = sb3("y1f", [128, D], F32)
                    y1b4 = sb3("y1b4", [128, 4 * D], BF16)
                    y1ts = [sb3(f"y1ts{i}", [128, 1024], BF16) for i in range(2)]
                    junk = sb3("junk", [128, D], BF16)
                    qtmp = sb3("qtmp", [1, 1024], F32)
                    ps_tp = [E3.enter_context(
                        nc.psum_tensor(f"ps_tp{p}_{i}", [128, 1024], BF16))
                             for i in range(2)]
                    ps_vb = [E3.enter_context(
                        nc.psum_tensor(f"ps_vb{p}_{i}", [128, 512], F32))
                             for i in range(2)]
                    ps_q = [E3.enter_context(
                        nc.psum_tensor(f"ps_q{p}_{i}", [1, 512], F32))
                            for i in range(2)]

                    wvr0 = ph.op("y", "dma_start", vvrow[:, :],
                                 bass.AP(vvrow_d, 0, [[C * 128, 1], [1, C * 128]]),
                                 inc=sY, incn=16)
                    vbmm, vbev = {}, {}
                    for grp in range(8):
                        bank = ps_vb[grp % 2]
                        for ci in range(4):
                            cc = grp * 4 + ci
                            w = []
                            if ci == 0 and grp == 0:
                                w.append((sY, wvr0))
                            if ci == 0 and grp >= 2:
                                w.append((sE, vbev[grp - 2]))
                            vbmm[grp] = ph.op(
                                "t", "matmul", bank[:, ci * 128:(ci + 1) * 128],
                                ones1f_s[:, :], vvrow[0:1, cc * 128:(cc + 1) * 128],
                                start=True, stop=True, wait=w, inc=sT)
                            ph.wait("t", sT, vbmm[grp])
                        for ci in range(4):
                            cc = grp * 4 + ci
                            c = invperm(cc)
                            vbev[grp] = ph.op("v", "tensor_copy",
                                              vvb[:, c * 128:(c + 1) * 128],
                                              bank[:, ci * 128:(ci + 1) * 128],
                                              wait=[(sT, vbmm[grp])] if ci == 0 else (),
                                              inc=sE)
                    xrd, s1dn, y1fd, y1bd = {}, {}, {}, {}
                    spl = {}
                    for k in range(4):
                        tg = p * 4 + k
                        w = [] if k == 0 else [(sE, s1dn[k - 1])]
                        ph.op("y", "dma_start",
                              AP(xrb, 0, [[D, 128], [1, D]]),
                              AP(xr, (p * 512 + k * 128) * D, [[D, 128], [1, D]]),
                              wait=w, inc=sX, incn=16)
                        xrd[k] = sX.n
                        w = [(sE, vbev[7])] if k == 0 else [(sE, y1fd[k - 1])]
                        wtt1 = ph.op("v", "tensor_tensor", s1[:, :],
                                     acc[:, k * D:(k + 1) * D], vvb[:, :], ALU.mult,
                                     wait=w, inc=sE)
                        ph.wait("v", sE, wtt1)
                        s1dn[k] = ph.op("v", "scalar_tensor_tensor", s1[:, :],
                                        s1[:, :], 1.0, xrb[:, :], ALU.mult, ALU.add,
                                        accum_out=stat1[:, tg * 4:tg * 4 + 1],
                                        wait=[(sX, xrd[k])], inc=sE)
                        wsqw = [(sE, s1dn[k])]
                        if k >= 1:
                            wsqw.append((sS, y1bd[k - 1]))
                        wsq = ph.op("s", "activation", junk[:, :], s1[:, :], AF.Square,
                                    accum_out=stat1[:, tg * 4 + 1:tg * 4 + 2],
                                    wait=wsqw, inc=sS)
                        vop(ph, "tensor_scalar", stat1[:, tg * 4 + 2:tg * 4 + 3],
                            stat1[:, tg * 4:tg * 4 + 1], -1.0 / D, None, ALU.mult,
                            wait=[(sS, wsq)])
                        vop(ph, "tensor_scalar", svar[:, 0:1],
                            stat1[:, tg * 4 + 1:tg * 4 + 2], 1.0 / D, None, ALU.mult)
                        vop(ph, "tensor_tensor", svar[:, 1:2],
                            stat1[:, tg * 4 + 2:tg * 4 + 3],
                            stat1[:, tg * 4 + 2:tg * 4 + 3], ALU.mult)
                        wvr = vop(ph, "tensor_tensor", svar[:, 0:1], svar[:, 0:1],
                                  svar[:, 1:2], ALU.subtract)
                        wsd = ph.op("s", "activation", svar[:, 1:2], svar[:, 0:1],
                                    AF.Sqrt, bias=epsc_s[:, :], wait=[(sE, wvr)], inc=sS)
                        vop(ph, "reciprocal", stat1[:, tg * 4 + 3:tg * 4 + 4],
                            svar[:, 1:2], wait=[(sS, wsd)])
                        w = []
                        if k >= 1:
                            w.append((sSp, spl[k - 1]))
                            w.append((sS, y1bd[k - 1]))
                        y1fd[k] = ph.op("v", "tensor_scalar", y1f[:, :], s1[:, :],
                                        stat1[:, tg * 4 + 2:tg * 4 + 3],
                                        stat1[:, tg * 4 + 3:tg * 4 + 4],
                                        ALU.add, ALU.mult, wait=w, inc=sE)
                        ph.op("y", "dma_start",
                              AP(y1sp, tg * 128 * D, [[D, 128], [1, D]]),
                              AP(y1f, 0, [[D, 128], [1, D]]),
                              wait=[(sE, y1fd[k])], inc=sSp, incn=16)
                        spl[k] = sSp.n
                        y1bd[k] = ph.op("s", "copy", y1b4[:, k * D:(k + 1) * D],
                                        y1f[:, :],
                                        wait=[(sE, y1fd[k])], inc=sS)
                    tpmm, tpev, tsd = {}, {}, {}
                    for cg in range(16):
                        bank = ps_tp[cg % 2]
                        for ci in range(2):
                            c = cg * 2 + ci
                            for tk in range(4):
                                w = []
                                if ci == 0 and tk == 0:
                                    if cg == 0:
                                        w.append((sS, y1bd[3]))
                                    if cg >= 2:
                                        w.append((sE, tpev[cg - 2]))
                                tpmm[cg] = tpose(
                                    ph,
                                    bank[:, (ci * 4 + tk) * 128:(ci * 4 + tk + 1) * 128],
                                    y1b4[:, tk * D + c * 128:tk * D + (c + 1) * 128],
                                    identb_s[:, :], wait=w)
                        w = [(sT, tpmm[cg])]
                        if cg >= 1:
                            w.append((sTs, tsd[cg - 1]))
                        tpev[cg] = ph.op("v", "tensor_copy", y1ts[cg % 2][:, :],
                                         bank[:, :], wait=w, inc=sE)
                        ph.op("y", "dma_start",
                              AP(y1td, (cg * 2) * 128 * 1024 + p * 512,
                                 [[1024, 128], [128 * 1024, 2], [1, 512]]),
                              AP(y1ts[cg % 2], 0, [[1024, 128], [512, 2], [1, 512]]),
                              wait=[(sE, tpev[cg])], inc=sTs, incn=16)
                        tsd[cg] = sTs.n
                    qmm, qev, qdm = {}, {}, {}
                    for fc in range(8):
                        bank = ps_q[fc % 2]
                        for tk in range(4):
                            w = []
                            if tk == 0 and fc >= 2:
                                w.append((sE, qev[fc - 2]))
                            ph.op("t", "matmul", bank[:, :],
                                  mq_s[:, p * 4 + tk:p * 4 + tk + 1],
                                  y1b4[:, tk * D + fc * 512:tk * D + (fc + 1) * 512],
                                  start=(tk == 0), stop=(tk == 3),
                                  wait=w, inc=sT if tk == 3 else None)
                        qmm[fc] = sT.n
                        w = [(sT, qmm[fc])]
                        if fc >= 1:
                            w.append((sQt, qdm[fc - 1]))
                        qev[fc] = ph.op("v", "tensor_copy",
                                        qtmp[0:1, (fc % 2) * 512:(fc % 2) * 512 + 512],
                                        bank[0:1, :], wait=w, inc=sE)
                        qd = qsb_d if p == 0 else qsa_d
                        ph.op("y", "dma_start",
                              qd[0:1, fc * 512:(fc + 1) * 512],
                              qtmp[0:1, (fc % 2) * 512:(fc % 2) * 512 + 512],
                              wait=[(sE, qev[fc])], inc=sQt, incn=16)
                        qdm[fc] = sQt.n
                    ph.wait("y", sY, sY.n)
                    ph.wait("y", sX, sX.n)
                    ph.wait("y", sSp, sSp.n)
                    ph.wait("y", sTs, sTs.n)
                    ph.wait("y", sQt, sQt.n)
                    ph.run()

            sweep(0, NG1)
            ln1(0)
            sweep(1, NG2)
            ln1(1)

        with contextlib.ExitStack() as EF:
            sbf = lambda n, sh, dt: EF.enter_context(nc.sbuf_tensor(n, sh, dt))
            y1T = sbf("y1T", [128, C * 1024], BF16)
            Abf = sbf("Abf", [128, C * 128], BF16)
            Bmbf = sbf("Bmbf", [128, D], BF16)
            rwtsb = sbf("rwtsb", [128, C * 8], BF16)
            bb16 = sbf("bb16", [1, D], BF16)
            tTs = sbf("tTs", [128, 1024], BF16)

            ph = Phase(nc)
            with contextlib.ExitStack() as E6:
                sb6 = lambda n, sh, dt: E6.enter_context(nc.sbuf_tensor(n, sh, dt))
                qrow = sb6("qrow", [1, D], F32)
                qrow2 = sb6("qrow2", [1, D], F32)
                qmf = sb6("qmf", [128, C], F32)
                qmfb = sb6("qmfb", [128, C], BF16)
                lg = sb6("lg", [1, 8], F32)
                mx = sb6("mx", [1, 1], F32)
                ex = sb6("ex", [1, 8], F32)
                sm = sb6("sm", [1, 1], F32)
                rt = sb6("rt", [1, 8], F32)
                r8 = sb6("r8", [8, 1], F32)
                ps_r = E6.enter_context(nc.psum_tensor("ps_r", [1, 8], F32))
                ps_q2 = E6.enter_context(nc.psum_tensor("ps_q2", [128, C], F32))
                ps_r8 = E6.enter_context(nc.psum_tensor("ps_r8", [8, 1], F32))
                ps_r128 = E6.enter_context(nc.psum_tensor("ps_r128", [128, 1], F32))

                for cs in range(8):
                    ph.op("g", "dma_start",
                          AP(y1T, cs * 4 * 1024, [[C * 1024, 128], [1024, 4], [1, 1024]]),
                          AP(y1td, cs * 4 * 128 * 1024,
                             [[1024, 128], [128 * 1024, 4], [1, 1024]]),
                          inc=sG, incn=16)
                ph.op("g", "dma_start", Abf[:, :],
                      AP(ats, 0, [[C * 128, 128], [1, C * 128]]), inc=sG, incn=16)
                ph.op("g", "dma_start", Bmbf[:, :], bm.ap(), inc=sG, incn=16)
                ph.op("g", "dma_start", rwtsb[:, :],
                      AP(rwts, 0, [[C * 8, 128], [1, C * 8]]), inc=sG, incn=16)
                ph.op("g", "dma_start", bb16[:, :], biasr.ap(), inc=sG, incn=16)
                wffl = sG.n

                ph.op("y", "dma_start", qrow[:, :], qsb_d.ap(), inc=sY, incn=16)
                ph.op("y", "dma_start", qrow2[:, :], qsa_d.ap(), inc=sY, incn=16)
                wq2 = sY.n
                wqa = ph.op("v", "tensor_tensor", qrow[:, :], qrow[:, :], qrow2[:, :],
                            ALU.add, wait=[(sY, wq2)], inc=sE)
                wcci = ph.op("y", "dma_start", ccin.ap(), qrow[:, :],
                             wait=[(sE, wqa)], inc=sY, incn=16)
                ph.op("g", "collective_compute", "AllReduce", ALU.add,
                      replica_groups=[list(range(NCORE))],
                      ins=[ccin.ap().opt()], outs=[ccout.ap().opt()],
                      wait=[(sY, wcci)], inc=sCC, incn=1)
                wccr = ph.op("y", "dma_start", qrow[:, :], ccout.ap(),
                             wait=[(sCC, 1)], inc=sY, incn=16)
                wqm = ph.op("v", "tensor_scalar", qrow[:, :], qrow[:, :],
                            1.0 / NVALID, None, ALU.mult,
                            wait=[(sY, wccr)], inc=sE)
                for j in range(C):
                    wq2t = tpose(ph, ps_q2[:, j:j + 1],
                                 qrow[0:1, j * 128:(j + 1) * 128], identf_s[0:1, 0:1],
                                 wait=[(sE, wqm)] if j == 0 else ())
                vop(ph, "tensor_copy", qmf[:, :], ps_q2[:, :], wait=[(sT, wq2t)])
                wqb = vop(ph, "tensor_copy", qmfb[:, :], qmf[:, :])
                for c in range(C):
                    w = []
                    if c == 0:
                        w = [(sE, wqb), (sG, wffl)]
                    ph.op("t", "matmul", ps_r[:, :], qmfb[:, c:c + 1],
                          rwtsb[:, c * 8:(c + 1) * 8],
                          start=(c == 0), stop=(c == C - 1),
                          wait=w, inc=sT if c == C - 1 else None)
                wlg = sT.n
                vop(ph, "scalar_tensor_tensor", lg[:, :], ps_r[:, :], 1.0,
                    rbs_s[:, :], ALU.mult, ALU.add, wait=[(sT, wlg)])
                vop(ph, "reduce_max", mx[:, :], lg[:, :], AX)
                wmx = vop(ph, "tensor_scalar", mx[:, :], mx[:, :], -1.0, None,
                          ALU.mult)
                wex = ph.op("s", "activation", ex[:, :], lg[:, :], AF.Exp,
                            bias=mx[:, :], wait=[(sE, wmx)], inc=sS)
                vop(ph, "reduce_sum", sm[:, :], ex[:, :], AX, wait=[(sS, wex)])
                vop(ph, "reciprocal", sm[:, :], sm[:, :])
                wrt = vop(ph, "tensor_scalar", rt[:, :], ex[:, :], sm[:, :],
                          SCALING, ALU.mult, ALU.mult)
                wr8 = tpose(ph, ps_r8[:, :], rt[0:1, :],
                            identf_s[0:1, 0:1], wait=[(sE, wrt)])
                wr8e = ph.op("v", "tensor_copy", r8[:, :], ps_r8[:, :],
                             wait=[(sT, wr8)], inc=sE)
                wrmm = ph.op("t", "matmul", ps_r128[:, :], rep8_s[:, :], r8[:, :],
                             start=True, stop=True, wait=[(sE, wr8e)], inc=sT)
                wrout = ph.op("v", "tensor_copy", rout128[:, :], ps_r128[:, :],
                              wait=[(sT, wrmm)], inc=sE)
                ph.run()

            ph = Phase(nc)
            with contextlib.ExitStack() as E7:
                ps_tt = [E7.enter_context(nc.psum_tensor(f"ps_tt{i}", [128, 512], F32))
                         for i in range(2)]
                for tb in range(2):
                    for c in range(C):
                        w = [(sG, wffl)] if (tb == 0 and c == 0) else []
                        ph.op("t", "matmul", ps_tt[tb][:, :],
                              Abf[:, c * 128:(c + 1) * 128],
                              y1T[:, c * 1024 + tb * 512:c * 1024 + tb * 512 + 512],
                              start=(c == 0), stop=(c == C - 1),
                              wait=w, inc=sT if c == C - 1 else None)
                    wtt = sT.n
                    ph.op("v", "tensor_scalar", tTs[:, tb * 512:(tb + 1) * 512],
                          ps_tt[tb][:, :], rout128[:, :], None, ALU.mult,
                          wait=[(sT, wtt), (sE, wrout)], inc=sE)
                wtts = sE.n
                ph.run()

            ph = Phase(nc)
            with contextlib.ExitStack() as E8:
                sb8 = lambda n, sh, dt: E8.enter_context(nc.sbuf_tensor(n, sh, dt))
                Wo = [sb8(f"Wo{i}", [128, C * 512], BF16) for i in range(2)]
                stmp = [sb8(f"stmp{i}", [128, 512], F32) for i in range(4)]
                ps_bs = [E8.enter_context(nc.psum_tensor(f"ps_bs{i}", [128, 512], F32))
                         for i in range(8)]

                wW, bsd, stev, stw = {}, {}, {}, {}

                def emit_w(o):
                    sWp = sW0 if o % 2 == 0 else sW1
                    w = []
                    if o >= 2:
                        w = [(sT, bsd[(o - 2, 7)])]
                    for dg in range(8):
                        ph.op("g", "dma_start",
                              AP(Wo[o % 2], dg * 4 * 512,
                                 [[C * 512, 128], [512, 4], [1, 512]]),
                              AP(wt, (dg * 4 * 128) * D + o * 512,
                                 [[D, 128], [128 * D, 4], [1, 512]]),
                              wait=w if dg == 0 else (), inc=sWp, incn=16)
                    wW[o] = (sWp, sWp.n)

                emit_w(0)
                emit_w(1)
                for o in range(8):
                    for half in range(2):
                        for dg in range(8):
                            for dc in range(4):
                                cch = dg * 4 + dc
                                for tt in range(4):
                                    tglob = half * 4 + tt
                                    w = []
                                    if dg == 0 and dc == 0 and tt == 0:
                                        w.append(wW[o])
                                        if half == 0 and o == 0:
                                            w.append((sE, wtts))
                                        if o >= 1:
                                            w.append((sE, stev[(o - 1, half * 4 + 3)]))
                                    ph.op("t", "matmul", ps_bs[tglob][:, :],
                                          y1T[:, cch * 1024 + tglob * 128:
                                              cch * 1024 + tglob * 128 + 128],
                                          Wo[o % 2][:, cch * 512:(cch + 1) * 512],
                                          start=(dg == 0 and dc == 0), stop=False,
                                          wait=w)
                        for tt in range(4):
                            tglob = half * 4 + tt
                            ph.op("t", "matmul", ps_bs[tglob][:, :],
                                  tTs[:, tglob * 128:(tglob + 1) * 128],
                                  Bmbf[:, o * 512:(o + 1) * 512],
                                  start=False, stop=False)
                            ph.op("t", "matmul", ps_bs[tglob][:, :],
                                  ones1b_s[:, :], bb16[0:1, o * 512:(o + 1) * 512],
                                  start=False, stop=True, inc=sT)
                            bsd[(o, tglob)] = sT.n
                        for tt in range(4):
                            tglob = half * 4 + tt
                            w = [(sT, bsd[(o, tglob)])]
                            ko = (o * 8 + tglob)
                            if ko >= 1:
                                w.append((sSd, stw[ko - 1]))
                            stev[(o, tglob)] = ph.op("v", "tensor_copy",
                                                     stmp[tglob % 4][:, :],
                                                     ps_bs[tglob][:, :], wait=w, inc=sE)
                            ph.op("y", "dma_start",
                                  AP(spd, tglob * 128 * D + o * 512,
                                     [[D, 128], [1, 512]]),
                                  AP(stmp[tglob % 4], 0, [[512, 128], [1, 512]]),
                                  wait=[(sE, stev[(o, tglob)])], inc=sSd, incn=16)
                            stw[ko] = sSd.n
                    if o + 2 < 8:
                        emit_w(o + 2)
                ph.wait("y", sSd, sSd.n)
                ph.run()

        ph = Phase(nc)
        with contextlib.ExitStack() as E9:
            sb9 = lambda n, sh, dt: E9.enter_context(nc.sbuf_tensor(n, sh, dt))
            spb = [sb9(f"spb{i}", [128, D], F32) for i in range(2)]
            y1rb = [sb9(f"y1rb{i}", [128, D], F32) for i in range(2)]
            s2 = sb9("s2", [128, D], F32)
            o2 = [sb9(f"o2{i}", [128, D], F32) for i in range(2)]
            junk2 = sb9("junk2", [128, D], BF16)

            spr, y1r, s2dn, odn, outd = {}, {}, {}, {}, {}
            wsd_prev = [0]
            for tg in range(8):
                sRp = sR0 if tg % 2 == 0 else sR1
                w = []
                if tg >= 2:
                    w = [(sE, s2dn[tg - 2])]
                ph.op("y", "dma_start", AP(spb[tg % 2], 0, [[D, 128], [1, D]]),
                      AP(spd, tg * 128 * D, [[D, 128], [1, D]]),
                      wait=w, inc=sRp, incn=16)
                ph.op("y", "dma_start", AP(y1rb[tg % 2], 0, [[D, 128], [1, D]]),
                      AP(y1sp, tg * 128 * D, [[D, 128], [1, D]]),
                      inc=sRp, incn=16)
                y1r[tg] = sRp.n
                ws2 = [(sRp, y1r[tg])]
                if tg >= 1:
                    ws2.append((sE, odn[tg - 1]))
                s2dn[tg] = ph.op("v", "scalar_tensor_tensor", s2[:, :],
                                 spb[tg % 2][:, :], 1.0, y1rb[tg % 2][:, :],
                                 ALU.mult, ALU.add,
                                 accum_out=stat2[:, tg * 4:tg * 4 + 1],
                                 wait=ws2, inc=sE)
                wsqw = [(sE, s2dn[tg])]
                if tg >= 1:
                    wsqw.append((sS, wsd_prev[0]))
                wsq = ph.op("s", "activation", junk2[:, :], s2[:, :], AF.Square,
                            accum_out=stat2[:, tg * 4 + 1:tg * 4 + 2],
                            wait=wsqw, inc=sS)
                vop(ph, "tensor_scalar", stat2[:, tg * 4 + 2:tg * 4 + 3],
                    stat2[:, tg * 4:tg * 4 + 1], -1.0 / D, None, ALU.mult,
                    wait=[(sS, wsq)])
                vop(ph, "tensor_scalar", svar[:, 0:1],
                    stat2[:, tg * 4 + 1:tg * 4 + 2], 1.0 / D, None, ALU.mult)
                vop(ph, "tensor_tensor", svar[:, 1:2],
                    stat2[:, tg * 4 + 2:tg * 4 + 3],
                    stat2[:, tg * 4 + 2:tg * 4 + 3], ALU.mult)
                wvr = vop(ph, "tensor_tensor", svar[:, 0:1], svar[:, 0:1],
                          svar[:, 1:2], ALU.subtract)
                wsd = ph.op("s", "activation", svar[:, 1:2], svar[:, 0:1], AF.Sqrt,
                            bias=epsc_s[:, :], wait=[(sE, wvr)], inc=sS)
                wsd_prev[0] = wsd
                vop(ph, "reciprocal", stat2[:, tg * 4 + 3:tg * 4 + 4], svar[:, 1:2],
                    wait=[(sS, wsd)])
                w = []
                if tg >= 1:
                    w.append((sO, outd[tg - 1]))
                odn[tg] = ph.op("v", "tensor_scalar", o2[tg % 2][:, :], s2[:, :],
                                stat2[:, tg * 4 + 2:tg * 4 + 3],
                                stat2[:, tg * 4 + 3:tg * 4 + 4],
                                ALU.add, ALU.mult, wait=w, inc=sE)
                ph.op("y", "dma_start", AP(out, tg * 128 * D, [[D, 128], [1, D]]),
                      AP(o2[tg % 2], 0, [[D, 128], [1, D]]),
                      wait=[(sE, odn[tg])], inc=sO, incn=16)
                outd[tg] = sO.n
            ph.wait("y", sO, sO.n)
            ph.run()

    return nc


_NC_CACHE = [None]


def _masks(core):
    tri = np.zeros((GK, 128, 512), np.float32)
    for j in range(GK):
        ii = np.arange(128)[:, None]
        qq = np.arange(512)[None, :]
        tri[j] = (qq >= j * 128 + ii).astype(np.float32)
    ones_m = np.ones((128, 512), np.float32)
    zeros_m = np.zeros((128, 512), np.float32)
    m = np.zeros((NG1 + NG2, 128, GK, 512), np.float32)
    for g in range(NG1):
        diag = 15 - core
        for j in range(GK):
            m[g, :, j, :] = ones_m if g < diag else (tri[j] if g == diag else zeros_m)
    for gg in range(NG2):
        diag = core
        for j in range(GK):
            m[NG1 + gg, :, j, :] = (ones_m if gg < diag else
                                    (tri[j] if gg == diag else zeros_m))
    return m.astype(BF)


def _rearr_q(rows):
    return np.ascontiguousarray(rows.T.reshape(C, 128, 512).transpose(1, 0, 2))


def kernel(L0_attn_x, L0_attn_bv_q, L0_attn_bv_k, L0_attn_bv_v,
           L0_ffn_W, L0_ffn_b, L0_ffn_A, L0_ffn_B,
           L0_ffn_router_W, L0_ffn_router_b):
    X = np.ascontiguousarray(np.asarray(L0_attn_x, np.float32).reshape(T, D))
    XT = np.ascontiguousarray(X.T)
    WT = np.ascontiguousarray(np.asarray(L0_ffn_W, np.float32).T)
    AT = np.asarray(L0_ffn_A, np.float32).reshape(128, D).T
    ATS = np.ascontiguousarray(AT.reshape(C, 128, 128).transpose(1, 0, 2))
    BM = np.ascontiguousarray(
        np.asarray(L0_ffn_B, np.float32).transpose(0, 2, 1).reshape(128, D))
    RWT = np.asarray(L0_ffn_router_W, np.float32).T
    RWTS = np.ascontiguousarray(RWT.reshape(C, 128, 8).transpose(1, 0, 2))
    rb = np.ascontiguousarray(np.asarray(L0_ffn_router_b, np.float32).reshape(1, 8))
    bias = np.ascontiguousarray(np.asarray(L0_ffn_b, np.float32).reshape(1, D))
    bvqa = np.ascontiguousarray(np.asarray(L0_attn_bv_q, np.float32))
    bvka = np.ascontiguousarray(np.asarray(L0_attn_bv_k, np.float32))
    bvva = np.ascontiguousarray(np.asarray(L0_attn_bv_v, np.float32))

    shared = dict(
        x=X, xt=XT, wt=WT, ats=ATS, bm=BM, rwts=RWTS, rbs=rb, biasr=bias,
        bvq=bvqa, bvk=bvka, bvv=bvva,
        identb=np.eye(128, dtype=BF),
        identf=np.eye(128, dtype=np.float32),
        ones1f=np.ones((1, 128), np.float32),
        ones1b=np.ones((1, 128), BF),
        rep8=np.repeat(np.eye(8, dtype=np.float32), 16, axis=1),
        epsc=np.full((128, 1), EPS, np.float32),
    )
    in_maps = []
    for c in range(NCORE):
        B0 = 512 * (15 - c)
        A0 = 512 * c
        q = np.stack([_rearr_q(X[B0:B0 + 512]), _rearr_q(X[A0:A0 + 512])])
        xrh = np.stack([X[B0:B0 + 512], X[A0:A0 + 512]])
        loc = np.concatenate([np.arange(B0, B0 + 512), np.arange(A0, A0 + 512)])
        mqv = ((loc >= Q0) & (loc < Q1)).astype(np.float32)
        mqs = np.ascontiguousarray(mqv.reshape(8, 128).T).astype(BF)
        m = dict(shared)
        m.update(qsrc=q, xr=np.ascontiguousarray(xrh), msk=_masks(c), mq=mqs)
        in_maps.append(m)

    if _NC_CACHE[0] is None:
        _NC_CACHE[0] = build_program()
    nc = _NC_CACHE[0]
    res = run_bass_kernel_spmd(nc, in_maps, core_ids=list(range(NCORE)))

    full = np.empty((1, T, D), np.float32)
    for c in range(NCORE):
        r = res.results[c]["out"]
        B0 = 512 * (15 - c)
        A0 = 512 * c
        full[0, B0:B0 + 512] = r[0:512]
        full[0, A0:A0 + 512] = r[512:1024]
    return full


# version 12
# speedup vs baseline: 1.1610x; 1.1610x over previous
import contextlib
import numpy as np
import ml_dtypes
import concourse.bass as bass
import concourse.mybir as mybir
from concourse.bass_utils import run_bass_kernel_spmd

F32 = mybir.dt.float32
BF16 = mybir.dt.bfloat16
AF = mybir.ActivationFunctionType
ALU = mybir.AluOpType
AX = mybir.AxisListType.X
BF = ml_dtypes.bfloat16

T, D, H, HD = 8192, 4096, 4, 1024
C = 32
NG1, NG2 = 16, 8
GK = 4
NCORE = 8
EPS = 1e-5
Q0, Q1 = 611, 8191
NVALID = Q1 - Q0
SCALING = 2.0


def colperm(c):
    return (c % 8) * 4 + (c // 8)


def invperm(cc):
    return 8 * (cc % 4) + cc // 4


class Sem:
    def __init__(self, es, nc, name):
        self.h = es.enter_context(nc.semaphore(name))
        self.n = 0


class Phase:

    def __init__(self, nc):
        self.nc = nc
        self.q = {k: [] for k in "tvsgy"}

    def op(self, k, method, *args, wait=(), inc=None, incn=1, **kwargs):
        tgt = None
        if inc is not None:
            inc.n += incn
            tgt = inc.n
        waits = [(s.h, v) for (s, v) in wait]

        def f(eng, method=method, args=args, kwargs=kwargs, waits=waits,
              inc=inc, incn=incn):
            for h, v in waits:
                eng.wait_ge(h, v)
            ins = getattr(eng, method)(*args, **kwargs)
            if inc is not None:
                ins.then_inc(inc.h, incn)
        self.q[k].append(f)
        return tgt

    def wait(self, k, sem, val):
        self.q[k].append(lambda eng, h=sem.h, v=val: eng.wait_ge(h, v))

    def run(self):
        with self.nc.Block() as block:
            for key, starter in (("g", block.gpsimd), ("y", block.sync),
                                 ("t", block.tensor), ("s", block.scalar),
                                 ("v", block.vector)):
                fns = self.q[key]
                if fns:
                    starter(lambda eng, fns=fns: [f(eng) for f in fns])

    def run_flat(self, engmap):
        for key in "gytsv":
            eng = engmap[key]
            for f in self.q[key]:
                f(eng)


def build_program():
    nc = bass.Bass("TRN2", target_bir_lowering=False, num_devices=NCORE)
    AP = bass.AP

    x = nc.dram_tensor("x", [T, D], F32, kind="ExternalInput")
    xt = nc.dram_tensor("xt", [D, T], F32, kind="ExternalInput")
    qsrc = nc.dram_tensor("qsrc", [2, 128, C, 512], F32, kind="ExternalInput")
    xr = nc.dram_tensor("xr", [2, 512, D], F32, kind="ExternalInput")
    mq = nc.dram_tensor("mq", [128, 8], BF16, kind="ExternalInput")
    trim = nc.dram_tensor("trim", [128, GK * 512], BF16, kind="ExternalInput")
    wt = nc.dram_tensor("wt", [D, D], F32, kind="ExternalInput")
    ats = nc.dram_tensor("ats", [128, C, 128], F32, kind="ExternalInput")
    bm = nc.dram_tensor("bm", [128, D], F32, kind="ExternalInput")
    rwts = nc.dram_tensor("rwts", [128, C, 8], F32, kind="ExternalInput")
    rbs = nc.dram_tensor("rbs", [1, 8], F32, kind="ExternalInput")
    biasr = nc.dram_tensor("biasr", [1, D], F32, kind="ExternalInput")
    bvq = nc.dram_tensor("bvq", [4, HD], F32, kind="ExternalInput")
    bvk = nc.dram_tensor("bvk", [4, HD], F32, kind="ExternalInput")
    bvv = nc.dram_tensor("bvv", [4, HD], F32, kind="ExternalInput")
    identb = nc.dram_tensor("identb", [128, 128], BF16, kind="ExternalInput")
    identf = nc.dram_tensor("identf", [128, 128], F32, kind="ExternalInput")
    ones1f = nc.dram_tensor("ones1f", [1, 128], F32, kind="ExternalInput")
    ones1b = nc.dram_tensor("ones1b", [1, 128], BF16, kind="ExternalInput")
    rep8 = nc.dram_tensor("rep8", [8, 128], F32, kind="ExternalInput")
    epsc = nc.dram_tensor("epsc", [128, 1], F32, kind="ExternalInput")

    out = nc.dram_tensor("out", [1024, D], F32, kind="ExternalOutput")

    y1sp = nc.dram_tensor("y1sp", [8, 128, D], F32, kind="Internal")
    y1td = nc.dram_tensor("y1td", [C, 128, 1024], BF16, kind="Internal")
    spd = nc.dram_tensor("spd", [8, 128, D], F32, kind="Internal")
    qsb_d = nc.dram_tensor("qsb_d", [1, D], F32, kind="Internal")
    qsa_d = nc.dram_tensor("qsa_d", [1, D], F32, kind="Internal")
    vvrow_d = nc.dram_tensor("vvrow_d", [C, 128], F32, kind="Internal")
    ccin = nc.dram_tensor("ccin", [1, D], F32, kind="Internal")
    ccout = nc.dram_tensor("ccout", [1, D], F32, kind="Internal")

    ES = contextlib.ExitStack()
    with ES:
        sbp = lambda name, shape, dt: ES.enter_context(nc.sbuf_tensor(name, shape, dt))
        identb_s = sbp("identb_s", [128, 128], BF16)
        identf_s = sbp("identf_s", [128, 128], F32)
        ones1f_s = sbp("ones1f_s", [1, 128], F32)
        ones1b_s = sbp("ones1b_s", [1, 128], BF16)
        rep8_s = sbp("rep8_s", [8, 128], F32)
        mq_s = sbp("mq_s", [128, 8], BF16)
        qkv32 = sbp("qkv32", [128, C], F32)
        stat1 = sbp("stat1", [128, 32], F32)
        stat2 = sbp("stat2", [128, 32], F32)
        svar = sbp("svar", [128, 2], F32)
        rout128 = sbp("rout128", [128, 1], F32)
        rbs_s = sbp("rbs_s", [1, 8], F32)
        epsc_s = sbp("epsc_s", [128, 1], F32)
        tri_s = sbp("tri_s", [128, GK * 512], BF16)

        sK = Sem(ES, nc, "sK")
        sV = Sem(ES, nc, "sVd")
        sG = Sem(ES, nc, "sG")
        sY = Sem(ES, nc, "sY")
        sT = Sem(ES, nc, "sT")
        sS = Sem(ES, nc, "sS")
        sE = Sem(ES, nc, "sE")
        sCC = Sem(ES, nc, "sCC")
        sX = Sem(ES, nc, "sX")
        sSp = Sem(ES, nc, "sSp")
        sTs = Sem(ES, nc, "sTs")
        sQt = Sem(ES, nc, "sQt")
        sW0 = Sem(ES, nc, "sW0")
        sW1 = Sem(ES, nc, "sW1")
        sSd = Sem(ES, nc, "sSd")
        sR0 = Sem(ES, nc, "sR0")
        sR1 = Sem(ES, nc, "sR1")
        sO = Sem(ES, nc, "sO")
        sBar = Sem(ES, nc, "sBar")

        def vop(ph, method, *args, wait=(), **kw):
            v = ph.op("v", method, *args, wait=wait, inc=sE, **kw)
            ph.wait("v", sE, v)
            return v

        def tpose(ph, out_ap, in_ap, id_ap, wait=()):
            v = ph.op("t", "transpose", out_ap, in_ap, id_ap, wait=wait, inc=sT)
            ph.wait("t", sT, v)
            return v

        ph = Phase(nc)
        for dst, src in ((identb_s, identb), (identf_s, identf), (ones1f_s, ones1f),
                         (ones1b_s, ones1b), (rep8_s, rep8), (mq_s, mq), (rbs_s, rbs),
                         (epsc_s, epsc), (tri_s, trim)):
            ph.op("y", "dma_start", dst[:, :], src.ap(), inc=sY, incn=16)
        ph.wait("y", sY, sY.n)
        ph.run()

        ph = Phase(nc)
        with contextlib.ExitStack() as E1:
            sb1 = lambda name, shape, dt: E1.enter_context(nc.sbuf_tensor(name, shape, dt))
            bv = sb1("bv", [4, 3 * HD], F32)
            hard = sb1("hard", [4, 3 * HD], F32)
            alpha = sb1("alpha", [4, 3], F32)
            tscr = sb1("tscr", [4, HD], F32)
            qkvhm = sb1("qkvhm", [4, HD], F32)
            vv32 = sb1("vv32", [128, C], F32)
            vvT32 = sb1("vvT32", [C, 128], F32)
            ps_b = E1.enter_context(nc.psum_tensor("ps_b", [128, C], F32))
            ps_c = E1.enter_context(nc.psum_tensor("ps_c", [C, 128], F32))

            for i, src in enumerate((bvq, bvk, bvv)):
                ph.op("y", "dma_start", bv[:, i * HD:(i + 1) * HD], src.ap(),
                      inc=sY, incn=16)
            wld = sY.n
            for i in range(3):
                sl = bv[:, i * HD:(i + 1) * HD]
                hsl = hard[:, i * HD:(i + 1) * HD]
                vop(ph, "reduce_sum", alpha[:, i:i + 1], sl, AX,
                    apply_absolute_value=True, wait=[(sY, wld)])
                vop(ph, "tensor_scalar", alpha[:, i:i + 1], alpha[:, i:i + 1],
                    1.0 / HD, None, ALU.mult)
                wsg = ph.op("s", "activation", hsl, sl, AF.Sign,
                            wait=[(sY, wld)], inc=sS)
                vop(ph, "tensor_tensor", tscr[:, :], hsl, hsl, ALU.mult,
                    wait=[(sS, wsg)])
                vop(ph, "tensor_scalar", tscr[:, :], tscr[:, :], -1.0, 1.0,
                    ALU.mult, ALU.add)
                vop(ph, "tensor_tensor", hsl, hsl, tscr[:, :], ALU.add)
                vop(ph, "tensor_scalar", hsl, hsl, alpha[:, i:i + 1], None, ALU.mult)
            wqk = vop(ph, "tensor_tensor", qkvhm[:, :], hard[:, 0:HD],
                      hard[:, HD:2 * HD], ALU.mult)
            for j in range(8):
                w1 = tpose(ph, ps_b[:, 4 * j:4 * j + 4],
                           qkvhm[:, j * 128:(j + 1) * 128], identf_s[0:4, 0:4],
                           wait=[(sE, wqk)] if j == 0 else ())
            wcp1 = ph.op("v", "tensor_copy", qkv32[:, :], ps_b[:, :],
                         wait=[(sT, w1)], inc=sE)
            for j in range(8):
                w2 = tpose(ph, ps_b[:, 4 * j:4 * j + 4],
                           hard[:, 2 * HD + j * 128:2 * HD + (j + 1) * 128],
                           identf_s[0:4, 0:4],
                           wait=[(sE, wcp1)] if j == 0 else ())
            wcp2 = ph.op("v", "tensor_copy", vv32[:, :], ps_b[:, :],
                         wait=[(sT, w2)], inc=sE)
            w3 = tpose(ph, ps_c[:, :], vv32[:, :], identf_s[:, :],
                       wait=[(sE, wcp2)])
            wvt = ph.op("v", "tensor_copy", vvT32[:, :], ps_c[:, :],
                        wait=[(sT, w3)], inc=sE)
            ph.op("y", "dma_start", vvrow_d.ap(), vvT32[:, :],
                  wait=[(sE, wvt)], inc=sY, incn=16)
            ph.wait("y", sY, sY.n)
            ph.run()

        with contextlib.ExitStack() as EA:
            acc = EA.enter_context(nc.sbuf_tensor("acc", [128, 4 * D], F32))

            def sweep(sweep_i, NG, diag, body):
                ph = Phase(nc)
                with contextlib.ExitStack() as E2:
                    sb2 = lambda n, sh, dt: E2.enter_context(
                        nc.sbuf_tensor(f"{n}_{sweep_i}_{body}", sh, dt))
                    Qs = sb2("Qs", [128, C * 512], BF16)
                    wqs = None
                    with nc.sbuf_tensor(f"Qraw_{sweep_i}_{body}", [128, C * 512], F32) as Qraw:
                        wqr = ph.op("g", "dma_start",
                                    AP(Qraw, 0, [[C * 512, 128], [1, C * 512]]),
                                    AP(qsrc, sweep_i * 128 * C * 512,
                                       [[C * 512, 128], [1, C * 512]]),
                                    inc=sG, incn=16)
                        for c in range(C):
                            wqs = ph.op("v", "tensor_scalar",
                                        Qs[:, c * 512:(c + 1) * 512],
                                        Qraw[:, c * 512:(c + 1) * 512],
                                        qkv32[:, colperm(c):colperm(c) + 1], None,
                                        ALU.mult,
                                        wait=[(sG, wqr)] if c == 0 else (), inc=sE)
                    Kg = sb2("Kg", [128, C * 512], BF16)
                    Vg = sb2("Vg", [128, GK * D], BF16)
                    Pt = sb2("Pt", [128, 16 * 512], BF16)
                    ps_sc = [E2.enter_context(
                        nc.psum_tensor(f"ps_sc{sweep_i}_{body}_{i}", [128, 512], F32))
                             for i in range(2)]
                    ps_o = [E2.enter_context(
                        nc.psum_tensor(f"ps_o{sweep_i}_{body}_{i}", [128, 512], F32))
                            for i in range(4)]

                    wmz = ph.op("v", "memset", acc[:, :], 0.0, inc=sE)
                    ph.wait("v", sE, wmz)

                    kdone, vdone = {}, {}
                    sigval, mm1val, maskval, chval, addval = {}, {}, {}, {}, {}
                    ph1done, ph2done = {}, {}

                    def emit_dma(g):
                        w = [(sE, wqs)] if g == 0 else [(sT, ph1done[g - 1])]
                        for cs in range(8):
                            ph.op("g", "dma_start",
                                  AP(Kg, cs * 4 * 512,
                                     [[C * 512, 128], [512, 4], [1, 512]]),
                                  AP(xt, (cs * 4 * 128) * T + 512 * g,
                                     [[T, 128], [128 * T, 4], [1, 512]]),
                                  wait=w if cs == 0 else (), inc=sK, incn=16)
                        kdone[g] = sK.n
                        wv = [(sE, wqs)] if g == 0 else [(sT, ph2done[g - 1])]
                        ph.op("g", "dma_start",
                              AP(Vg, 0, [[GK * D, 128], [D, GK], [1, D]]),
                              AP(x, 512 * g * D, [[D, 128], [128 * D, GK], [1, D]]),
                              wait=wv, inc=sV, incn=16)
                        vdone[g] = sV.n

                    emit_dma(0)
                    ci_glob = 0
                    for g in range(NG):
                        for i in range(16):
                            j, h = i // 4, i % 4
                            gi = g * 16 + i
                            bank = ps_sc[gi % 2]
                            for cc in range(8):
                                c = 8 * h + cc
                                w = []
                                if cc == 0:
                                    if i == 0:
                                        w.append((sK, kdone[g]))
                                        if g == 0:
                                            w.append((sE, wqs))
                                    if gi >= 2:
                                        w.append((sS, sigval[gi - 2]))
                                ph.op("t", "matmul", bank[:, :],
                                      Kg[:, c * 512 + j * 128:c * 512 + j * 128 + 128],
                                      Qs[:, c * 512:(c + 1) * 512],
                                      start=(cc == 0), stop=(cc == 7),
                                      wait=w, inc=sT if cc == 7 else None)
                            mm1val[gi] = sT.n
                            w = [(sT, mm1val[gi])]
                            if i == 0 and g > 0:
                                w.append((sT, ph2done[g - 1]))
                            sigval[gi] = ph.op("s", "activation",
                                               Pt[:, i * 512:(i + 1) * 512],
                                               bank[:, :], AF.Sigmoid,
                                               scale=0.125, wait=w, inc=sS)
                            if g == diag:
                                maskval[(g, i)] = ph.op(
                                    "v", "tensor_tensor", Pt[:, i * 512:(i + 1) * 512],
                                    Pt[:, i * 512:(i + 1) * 512],
                                    tri_s[:, j * 512:(j + 1) * 512], ALU.mult,
                                    wait=[(sS, sigval[gi])], inc=sE)
                        ph1done[g] = sT.n
                        first_chain = True
                        for h in range(H):
                            for qs in range(4):
                                for dh in range(2):
                                    bank = ps_o[ci_glob % 4]
                                    for j in range(GK):
                                        w = []
                                        if j == 0:
                                            if first_chain:
                                                w.append((sV, vdone[g]))
                                                if g == diag:
                                                    w.append((sE, maskval[(g, 15)]))
                                                else:
                                                    w.append((sS, sigval[g * 16 + 15]))
                                            if ci_glob >= 4:
                                                w.append((sE, addval[ci_glob - 4]))
                                        ph.op("t", "matmul", bank[:, :],
                                              Pt[:, (4 * j + h) * 512 + qs * 128:
                                                 (4 * j + h) * 512 + qs * 128 + 128],
                                              Vg[:, j * D + h * HD + dh * 512:
                                                 j * D + h * HD + dh * 512 + 512],
                                              start=(j == 0), stop=(j == GK - 1),
                                              wait=w, inc=sT if j == GK - 1 else None)
                                    chval[ci_glob] = sT.n
                                    first_chain = False
                                    wadd = [(sT, chval[ci_glob])]
                                    if ci_glob >= 32:
                                        wadd.append((sE, addval[ci_glob - 32]))
                                    addval[ci_glob] = ph.op(
                                        "v", "tensor_tensor",
                                        acc[:, qs * D + h * HD + dh * 512:
                                            qs * D + h * HD + dh * 512 + 512],
                                        acc[:, qs * D + h * HD + dh * 512:
                                            qs * D + h * HD + dh * 512 + 512],
                                        bank[:, :], ALU.add,
                                        wait=wadd, inc=sE)
                                    ci_glob += 1
                        ph2done[g] = sT.n
                        if g + 1 < NG:
                            emit_dma(g + 1)
                    ph.run_flat(ENG)

            def ln1(p, body):
                ph = Phase(nc)
                with contextlib.ExitStack() as E3:
                    sb3 = lambda n, sh, dt: E3.enter_context(
                        nc.sbuf_tensor(f"{n}_{p}_{body}", sh, dt))
                    vvrow = sb3("vvrow", [1, D], F32)
                    vvb = sb3("vvb", [128, D], F32)
                    xrb = sb3("xrb", [128, D], F32)
                    s1 = sb3("s1", [128, D], F32)
                    y1f = sb3("y1f", [128, D], F32)
                    y1b4 = sb3("y1b4", [128, 4 * D], BF16)
                    y1ts = [sb3(f"y1ts{i}", [128, 1024], BF16) for i in range(2)]
                    junk = sb3("junk", [128, D], BF16)
                    qtmp = sb3("qtmp", [1, 1024], F32)
                    ps_tp = [E3.enter_context(
                        nc.psum_tensor(f"ps_tp{p}_{body}_{i}", [128, 1024], BF16))
                             for i in range(2)]
                    ps_vb = [E3.enter_context(
                        nc.psum_tensor(f"ps_vb{p}_{body}_{i}", [128, 512], F32))
                             for i in range(2)]
                    ps_q = [E3.enter_context(
                        nc.psum_tensor(f"ps_q{p}_{body}_{i}", [1, 512], F32))
                            for i in range(2)]

                    wvr0 = ph.op("y", "dma_start", vvrow[:, :],
                                 bass.AP(vvrow_d, 0, [[C * 128, 1], [1, C * 128]]),
                                 inc=sY, incn=16)
                    vbmm, vbev = {}, {}
                    for grp in range(8):
                        bank = ps_vb[grp % 2]
                        for ci in range(4):
                            cc = grp * 4 + ci
                            w = []
                            if ci == 0 and grp == 0:
                                w.append((sY, wvr0))
                            if ci == 0 and grp >= 2:
                                w.append((sE, vbev[grp - 2]))
                            vbmm[grp] = ph.op(
                                "t", "matmul", bank[:, ci * 128:(ci + 1) * 128],
                                ones1f_s[:, :], vvrow[0:1, cc * 128:(cc + 1) * 128],
                                start=True, stop=True, wait=w, inc=sT)
                            ph.wait("t", sT, vbmm[grp])
                        for ci in range(4):
                            cc = grp * 4 + ci
                            c = invperm(cc)
                            vbev[grp] = ph.op("v", "tensor_copy",
                                              vvb[:, c * 128:(c + 1) * 128],
                                              bank[:, ci * 128:(ci + 1) * 128],
                                              wait=[(sT, vbmm[grp])] if ci == 0 else (),
                                              inc=sE)
                    xrd, s1dn, y1fd, y1bd = {}, {}, {}, {}
                    spl = {}
                    for k in range(4):
                        tg = p * 4 + k
                        w = [] if k == 0 else [(sE, s1dn[k - 1])]
                        ph.op("y", "dma_start",
                              AP(xrb, 0, [[D, 128], [1, D]]),
                              AP(xr, (p * 512 + k * 128) * D, [[D, 128], [1, D]]),
                              wait=w, inc=sX, incn=16)
                        xrd[k] = sX.n
                        w = [(sE, vbev[7])] if k == 0 else [(sE, y1fd[k - 1])]
                        wtt1 = ph.op("v", "tensor_tensor", s1[:, :],
                                     acc[:, k * D:(k + 1) * D], vvb[:, :], ALU.mult,
                                     wait=w, inc=sE)
                        ph.wait("v", sE, wtt1)
                        s1dn[k] = ph.op("v", "scalar_tensor_tensor", s1[:, :],
                                        s1[:, :], 1.0, xrb[:, :], ALU.mult, ALU.add,
                                        accum_out=stat1[:, tg * 4:tg * 4 + 1],
                                        wait=[(sX, xrd[k])], inc=sE)
                        wsqw = [(sE, s1dn[k])]
                        if k >= 1:
                            wsqw.append((sS, y1bd[k - 1]))
                        wsq = ph.op("s", "activation", junk[:, :], s1[:, :], AF.Square,
                                    accum_out=stat1[:, tg * 4 + 1:tg * 4 + 2],
                                    wait=wsqw, inc=sS)
                        vop(ph, "tensor_scalar", stat1[:, tg * 4 + 2:tg * 4 + 3],
                            stat1[:, tg * 4:tg * 4 + 1], -1.0 / D, None, ALU.mult,
                            wait=[(sS, wsq)])
                        vop(ph, "tensor_scalar", svar[:, 0:1],
                            stat1[:, tg * 4 + 1:tg * 4 + 2], 1.0 / D, None, ALU.mult)
                        vop(ph, "tensor_tensor", svar[:, 1:2],
                            stat1[:, tg * 4 + 2:tg * 4 + 3],
                            stat1[:, tg * 4 + 2:tg * 4 + 3], ALU.mult)
                        wvr = vop(ph, "tensor_tensor", svar[:, 0:1], svar[:, 0:1],
                                  svar[:, 1:2], ALU.subtract)
                        wsd = ph.op("s", "activation", svar[:, 1:2], svar[:, 0:1],
                                    AF.Sqrt, bias=epsc_s[:, :], wait=[(sE, wvr)], inc=sS)
                        vop(ph, "reciprocal", stat1[:, tg * 4 + 3:tg * 4 + 4],
                            svar[:, 1:2], wait=[(sS, wsd)])
                        w = []
                        if k >= 1:
                            w.append((sSp, spl[k - 1]))
                            w.append((sS, y1bd[k - 1]))
                        y1fd[k] = ph.op("v", "tensor_scalar", y1f[:, :], s1[:, :],
                                        stat1[:, tg * 4 + 2:tg * 4 + 3],
                                        stat1[:, tg * 4 + 3:tg * 4 + 4],
                                        ALU.add, ALU.mult, wait=w, inc=sE)
                        ph.op("y", "dma_start",
                              AP(y1sp, tg * 128 * D, [[D, 128], [1, D]]),
                              AP(y1f, 0, [[D, 128], [1, D]]),
                              wait=[(sE, y1fd[k])], inc=sSp, incn=16)
                        spl[k] = sSp.n
                        y1bd[k] = ph.op("s", "copy", y1b4[:, k * D:(k + 1) * D],
                                        y1f[:, :],
                                        wait=[(sE, y1fd[k])], inc=sS)
                    tpmm, tpev, tsd = {}, {}, {}
                    for cg in range(16):
                        bank = ps_tp[cg % 2]
                        for ci in range(2):
                            c = cg * 2 + ci
                            for tk in range(4):
                                w = []
                                if ci == 0 and tk == 0:
                                    if cg == 0:
                                        w.append((sS, y1bd[3]))
                                    if cg >= 2:
                                        w.append((sE, tpev[cg - 2]))
                                tpmm[cg] = tpose(
                                    ph,
                                    bank[:, (ci * 4 + tk) * 128:(ci * 4 + tk + 1) * 128],
                                    y1b4[:, tk * D + c * 128:tk * D + (c + 1) * 128],
                                    identb_s[:, :], wait=w)
                        w = [(sT, tpmm[cg])]
                        if cg >= 1:
                            w.append((sTs, tsd[cg - 1]))
                        tpev[cg] = ph.op("v", "tensor_copy", y1ts[cg % 2][:, :],
                                         bank[:, :], wait=w, inc=sE)
                        ph.op("y", "dma_start",
                              AP(y1td, (cg * 2) * 128 * 1024 + p * 512,
                                 [[1024, 128], [128 * 1024, 2], [1, 512]]),
                              AP(y1ts[cg % 2], 0, [[1024, 128], [512, 2], [1, 512]]),
                              wait=[(sE, tpev[cg])], inc=sTs, incn=16)
                        tsd[cg] = sTs.n
                    qmm, qev, qdm = {}, {}, {}
                    for fc in range(8):
                        bank = ps_q[fc % 2]
                        for tk in range(4):
                            w = []
                            if tk == 0 and fc >= 2:
                                w.append((sE, qev[fc - 2]))
                            ph.op("t", "matmul", bank[:, :],
                                  mq_s[:, p * 4 + tk:p * 4 + tk + 1],
                                  y1b4[:, tk * D + fc * 512:tk * D + (fc + 1) * 512],
                                  start=(tk == 0), stop=(tk == 3),
                                  wait=w, inc=sT if tk == 3 else None)
                        qmm[fc] = sT.n
                        w = [(sT, qmm[fc])]
                        if fc >= 1:
                            w.append((sQt, qdm[fc - 1]))
                        qev[fc] = ph.op("v", "tensor_copy",
                                        qtmp[0:1, (fc % 2) * 512:(fc % 2) * 512 + 512],
                                        bank[0:1, :], wait=w, inc=sE)
                        qd = qsb_d if p == 0 else qsa_d
                        ph.op("y", "dma_start",
                              qd[0:1, fc * 512:(fc + 1) * 512],
                              qtmp[0:1, (fc % 2) * 512:(fc % 2) * 512 + 512],
                              wait=[(sE, qev[fc])], inc=sQt, incn=16)
                        qdm[fc] = sQt.n
                    ph.wait("y", sY, sY.n)
                    ph.wait("y", sX, sX.n)
                    ph.wait("y", sSp, sSp.n)
                    ph.wait("y", sTs, sTs.n)
                    ph.wait("y", sQt, sQt.n)
                    ph.run()

            engs = [nc.tensor, nc.vector, nc.scalar, nc.gpsimd, nc.sync]
            ENG = {"t": nc.tensor, "v": nc.vector, "s": nc.scalar,
                   "g": nc.gpsimd, "y": nc.sync}
            body_sems = [sK, sV, sG, sY, sX, sSp, sTs, sQt, sT, sS, sE]

            def swept(sweep_i):
                NGmax = NG1 if sweep_i == 0 else NG2
                CEIL = {id(sK): 128 * NGmax, id(sV): 16 * NGmax, id(sG): 16,
                        id(sT): 48 * NGmax, id(sS): 16 * NGmax,
                        id(sE): 1 + 32 + 16 + 32 * NGmax,
                        id(sY): 0, id(sX): 0, id(sSp): 0, id(sTs): 0, id(sQt): 0}
                snap = [(s, s.n) for s in body_sems]
                sbar0 = sBar.n
                idx = [e.partition_id() for e in engs]
                for body in nc.Switch(engines=engs, index=idx, n=8):
                    for s, v in snap:
                        s.n = v
                    sBar.n = sbar0
                    if sweep_i == 0:
                        sweep(0, 16 - body, 15 - body, body)
                    else:
                        sweep(1, body + 1, body, body)
                    for s, v0 in snap:
                        assert s.n - v0 <= CEIL[id(s)], (s.h.name, s.n - v0)
                        nc.gpsimd.wait_ge(s.h, s.n)
                        pad = v0 + CEIL[id(s)] - s.n
                        if pad:
                            nc.gpsimd.sem_inc(s.h, pad)
                    tgt = sBar.n + 5
                    for e in engs:
                        e.nop().then_inc(sBar.h, 1)
                    for e in engs:
                        e.wait_ge(sBar.h, tgt)
                    sBar.n = tgt
                for s, v0 in snap:
                    s.n = v0 + CEIL[id(s)]

            swept(0)
            ln1(0, 0)
            swept(1)
            ln1(1, 0)

        with contextlib.ExitStack() as EF:
            sbf = lambda n, sh, dt: EF.enter_context(nc.sbuf_tensor(n, sh, dt))
            y1T = sbf("y1T", [128, C * 1024], BF16)
            Abf = sbf("Abf", [128, C * 128], BF16)
            Bmbf = sbf("Bmbf", [128, D], BF16)
            rwtsb = sbf("rwtsb", [128, C * 8], BF16)
            bb16 = sbf("bb16", [1, D], BF16)
            tTs = sbf("tTs", [128, 1024], BF16)

            ph = Phase(nc)
            with contextlib.ExitStack() as E6:
                sb6 = lambda n, sh, dt: E6.enter_context(nc.sbuf_tensor(n, sh, dt))
                qrow = sb6("qrow", [1, D], F32)
                qrow2 = sb6("qrow2", [1, D], F32)
                qmf = sb6("qmf", [128, C], F32)
                qmfb = sb6("qmfb", [128, C], BF16)
                lg = sb6("lg", [1, 8], F32)
                mx = sb6("mx", [1, 1], F32)
                ex = sb6("ex", [1, 8], F32)
                sm = sb6("sm", [1, 1], F32)
                rt = sb6("rt", [1, 8], F32)
                r8 = sb6("r8", [8, 1], F32)
                ps_r = E6.enter_context(nc.psum_tensor("ps_r", [1, 8], F32))
                ps_q2 = E6.enter_context(nc.psum_tensor("ps_q2", [128, C], F32))
                ps_r8 = E6.enter_context(nc.psum_tensor("ps_r8", [8, 1], F32))
                ps_r128 = E6.enter_context(nc.psum_tensor("ps_r128", [128, 1], F32))

                for cs in range(8):
                    ph.op("g", "dma_start",
                          AP(y1T, cs * 4 * 1024, [[C * 1024, 128], [1024, 4], [1, 1024]]),
                          AP(y1td, cs * 4 * 128 * 1024,
                             [[1024, 128], [128 * 1024, 4], [1, 1024]]),
                          inc=sG, incn=16)
                ph.op("g", "dma_start", Abf[:, :],
                      AP(ats, 0, [[C * 128, 128], [1, C * 128]]), inc=sG, incn=16)
                ph.op("g", "dma_start", Bmbf[:, :], bm.ap(), inc=sG, incn=16)
                ph.op("g", "dma_start", rwtsb[:, :],
                      AP(rwts, 0, [[C * 8, 128], [1, C * 8]]), inc=sG, incn=16)
                ph.op("g", "dma_start", bb16[:, :], biasr.ap(), inc=sG, incn=16)
                wffl = sG.n

                ph.op("y", "dma_start", qrow[:, :], qsb_d.ap(), inc=sY, incn=16)
                ph.op("y", "dma_start", qrow2[:, :], qsa_d.ap(), inc=sY, incn=16)
                wq2 = sY.n
                wqa = ph.op("v", "tensor_tensor", qrow[:, :], qrow[:, :], qrow2[:, :],
                            ALU.add, wait=[(sY, wq2)], inc=sE)
                wcci = ph.op("y", "dma_start", ccin.ap(), qrow[:, :],
                             wait=[(sE, wqa)], inc=sY, incn=16)
                ph.op("g", "collective_compute", "AllReduce", ALU.add,
                      replica_groups=[list(range(NCORE))],
                      ins=[ccin.ap().opt()], outs=[ccout.ap().opt()],
                      wait=[(sY, wcci)], inc=sCC, incn=1)
                wccr = ph.op("y", "dma_start", qrow[:, :], ccout.ap(),
                             wait=[(sCC, 1)], inc=sY, incn=16)
                wqm = ph.op("v", "tensor_scalar", qrow[:, :], qrow[:, :],
                            1.0 / NVALID, None, ALU.mult,
                            wait=[(sY, wccr)], inc=sE)
                for j in range(C):
                    wq2t = tpose(ph, ps_q2[:, j:j + 1],
                                 qrow[0:1, j * 128:(j + 1) * 128], identf_s[0:1, 0:1],
                                 wait=[(sE, wqm)] if j == 0 else ())
                vop(ph, "tensor_copy", qmf[:, :], ps_q2[:, :], wait=[(sT, wq2t)])
                wqb = vop(ph, "tensor_copy", qmfb[:, :], qmf[:, :])
                for c in range(C):
                    w = []
                    if c == 0:
                        w = [(sE, wqb), (sG, wffl)]
                    ph.op("t", "matmul", ps_r[:, :], qmfb[:, c:c + 1],
                          rwtsb[:, c * 8:(c + 1) * 8],
                          start=(c == 0), stop=(c == C - 1),
                          wait=w, inc=sT if c == C - 1 else None)
                wlg = sT.n
                vop(ph, "scalar_tensor_tensor", lg[:, :], ps_r[:, :], 1.0,
                    rbs_s[:, :], ALU.mult, ALU.add, wait=[(sT, wlg)])
                vop(ph, "reduce_max", mx[:, :], lg[:, :], AX)
                wmx = vop(ph, "tensor_scalar", mx[:, :], mx[:, :], -1.0, None,
                          ALU.mult)
                wex = ph.op("s", "activation", ex[:, :], lg[:, :], AF.Exp,
                            bias=mx[:, :], wait=[(sE, wmx)], inc=sS)
                vop(ph, "reduce_sum", sm[:, :], ex[:, :], AX, wait=[(sS, wex)])
                vop(ph, "reciprocal", sm[:, :], sm[:, :])
                wrt = vop(ph, "tensor_scalar", rt[:, :], ex[:, :], sm[:, :],
                          SCALING, ALU.mult, ALU.mult)
                wr8 = tpose(ph, ps_r8[:, :], rt[0:1, :],
                            identf_s[0:1, 0:1], wait=[(sE, wrt)])
                wr8e = ph.op("v", "tensor_copy", r8[:, :], ps_r8[:, :],
                             wait=[(sT, wr8)], inc=sE)
                wrmm = ph.op("t", "matmul", ps_r128[:, :], rep8_s[:, :], r8[:, :],
                             start=True, stop=True, wait=[(sE, wr8e)], inc=sT)
                wrout = ph.op("v", "tensor_copy", rout128[:, :], ps_r128[:, :],
                              wait=[(sT, wrmm)], inc=sE)
                ph.run()

            ph = Phase(nc)
            with contextlib.ExitStack() as E7:
                ps_tt = [E7.enter_context(nc.psum_tensor(f"ps_tt{i}", [128, 512], F32))
                         for i in range(2)]
                for tb in range(2):
                    for c in range(C):
                        w = [(sG, wffl)] if (tb == 0 and c == 0) else []
                        ph.op("t", "matmul", ps_tt[tb][:, :],
                              Abf[:, c * 128:(c + 1) * 128],
                              y1T[:, c * 1024 + tb * 512:c * 1024 + tb * 512 + 512],
                              start=(c == 0), stop=(c == C - 1),
                              wait=w, inc=sT if c == C - 1 else None)
                    wtt = sT.n
                    ph.op("v", "tensor_scalar", tTs[:, tb * 512:(tb + 1) * 512],
                          ps_tt[tb][:, :], rout128[:, :], None, ALU.mult,
                          wait=[(sT, wtt), (sE, wrout)], inc=sE)
                wtts = sE.n
                ph.run()

            ph = Phase(nc)
            with contextlib.ExitStack() as E8:
                sb8 = lambda n, sh, dt: E8.enter_context(nc.sbuf_tensor(n, sh, dt))
                Wo = [sb8(f"Wo{i}", [128, C * 512], BF16) for i in range(2)]
                stmp = [sb8(f"stmp{i}", [128, 512], F32) for i in range(4)]
                ps_bs = [E8.enter_context(nc.psum_tensor(f"ps_bs{i}", [128, 512], F32))
                         for i in range(8)]

                wW, bsd, stev, stw = {}, {}, {}, {}

                def emit_w(o):
                    sWp = sW0 if o % 2 == 0 else sW1
                    w = []
                    if o >= 2:
                        w = [(sT, bsd[(o - 2, 7)])]
                    for dg in range(8):
                        ph.op("g", "dma_start",
                              AP(Wo[o % 2], dg * 4 * 512,
                                 [[C * 512, 128], [512, 4], [1, 512]]),
                              AP(wt, (dg * 4 * 128) * D + o * 512,
                                 [[D, 128], [128 * D, 4], [1, 512]]),
                              wait=w if dg == 0 else (), inc=sWp, incn=16)
                    wW[o] = (sWp, sWp.n)

                emit_w(0)
                emit_w(1)
                for o in range(8):
                    for half in range(2):
                        for dg in range(8):
                            for dc in range(4):
                                cch = dg * 4 + dc
                                for tt in range(4):
                                    tglob = half * 4 + tt
                                    w = []
                                    if dg == 0 and dc == 0 and tt == 0:
                                        w.append(wW[o])
                                        if half == 0 and o == 0:
                                            w.append((sE, wtts))
                                        if o >= 1:
                                            w.append((sE, stev[(o - 1, half * 4 + 3)]))
                                    ph.op("t", "matmul", ps_bs[tglob][:, :],
                                          y1T[:, cch * 1024 + tglob * 128:
                                              cch * 1024 + tglob * 128 + 128],
                                          Wo[o % 2][:, cch * 512:(cch + 1) * 512],
                                          start=(dg == 0 and dc == 0), stop=False,
                                          wait=w)
                        for tt in range(4):
                            tglob = half * 4 + tt
                            ph.op("t", "matmul", ps_bs[tglob][:, :],
                                  tTs[:, tglob * 128:(tglob + 1) * 128],
                                  Bmbf[:, o * 512:(o + 1) * 512],
                                  start=False, stop=False)
                            ph.op("t", "matmul", ps_bs[tglob][:, :],
                                  ones1b_s[:, :], bb16[0:1, o * 512:(o + 1) * 512],
                                  start=False, stop=True, inc=sT)
                            bsd[(o, tglob)] = sT.n
                        for tt in range(4):
                            tglob = half * 4 + tt
                            w = [(sT, bsd[(o, tglob)])]
                            ko = (o * 8 + tglob)
                            if ko >= 1:
                                w.append((sSd, stw[ko - 1]))
                            stev[(o, tglob)] = ph.op("v", "tensor_copy",
                                                     stmp[tglob % 4][:, :],
                                                     ps_bs[tglob][:, :], wait=w, inc=sE)
                            ph.op("y", "dma_start",
                                  AP(spd, tglob * 128 * D + o * 512,
                                     [[D, 128], [1, 512]]),
                                  AP(stmp[tglob % 4], 0, [[512, 128], [1, 512]]),
                                  wait=[(sE, stev[(o, tglob)])], inc=sSd, incn=16)
                            stw[ko] = sSd.n
                    if o + 2 < 8:
                        emit_w(o + 2)
                ph.wait("y", sSd, sSd.n)
                ph.run()

        ph = Phase(nc)
        with contextlib.ExitStack() as E9:
            sb9 = lambda n, sh, dt: E9.enter_context(nc.sbuf_tensor(n, sh, dt))
            spb = [sb9(f"spb{i}", [128, D], F32) for i in range(2)]
            y1rb = [sb9(f"y1rb{i}", [128, D], F32) for i in range(2)]
            s2 = sb9("s2", [128, D], F32)
            o2 = [sb9(f"o2{i}", [128, D], F32) for i in range(2)]
            junk2 = sb9("junk2", [128, D], BF16)

            spr, y1r, s2dn, odn, outd = {}, {}, {}, {}, {}
            wsd_prev = [0]
            for tg in range(8):
                sRp = sR0 if tg % 2 == 0 else sR1
                w = []
                if tg >= 2:
                    w = [(sE, s2dn[tg - 2])]
                ph.op("y", "dma_start", AP(spb[tg % 2], 0, [[D, 128], [1, D]]),
                      AP(spd, tg * 128 * D, [[D, 128], [1, D]]),
                      wait=w, inc=sRp, incn=16)
                ph.op("y", "dma_start", AP(y1rb[tg % 2], 0, [[D, 128], [1, D]]),
                      AP(y1sp, tg * 128 * D, [[D, 128], [1, D]]),
                      inc=sRp, incn=16)
                y1r[tg] = sRp.n
                ws2 = [(sRp, y1r[tg])]
                if tg >= 1:
                    ws2.append((sE, odn[tg - 1]))
                s2dn[tg] = ph.op("v", "scalar_tensor_tensor", s2[:, :],
                                 spb[tg % 2][:, :], 1.0, y1rb[tg % 2][:, :],
                                 ALU.mult, ALU.add,
                                 accum_out=stat2[:, tg * 4:tg * 4 + 1],
                                 wait=ws2, inc=sE)
                wsqw = [(sE, s2dn[tg])]
                if tg >= 1:
                    wsqw.append((sS, wsd_prev[0]))
                wsq = ph.op("s", "activation", junk2[:, :], s2[:, :], AF.Square,
                            accum_out=stat2[:, tg * 4 + 1:tg * 4 + 2],
                            wait=wsqw, inc=sS)
                vop(ph, "tensor_scalar", stat2[:, tg * 4 + 2:tg * 4 + 3],
                    stat2[:, tg * 4:tg * 4 + 1], -1.0 / D, None, ALU.mult,
                    wait=[(sS, wsq)])
                vop(ph, "tensor_scalar", svar[:, 0:1],
                    stat2[:, tg * 4 + 1:tg * 4 + 2], 1.0 / D, None, ALU.mult)
                vop(ph, "tensor_tensor", svar[:, 1:2],
                    stat2[:, tg * 4 + 2:tg * 4 + 3],
                    stat2[:, tg * 4 + 2:tg * 4 + 3], ALU.mult)
                wvr = vop(ph, "tensor_tensor", svar[:, 0:1], svar[:, 0:1],
                          svar[:, 1:2], ALU.subtract)
                wsd = ph.op("s", "activation", svar[:, 1:2], svar[:, 0:1], AF.Sqrt,
                            bias=epsc_s[:, :], wait=[(sE, wvr)], inc=sS)
                wsd_prev[0] = wsd
                vop(ph, "reciprocal", stat2[:, tg * 4 + 3:tg * 4 + 4], svar[:, 1:2],
                    wait=[(sS, wsd)])
                w = []
                if tg >= 1:
                    w.append((sO, outd[tg - 1]))
                odn[tg] = ph.op("v", "tensor_scalar", o2[tg % 2][:, :], s2[:, :],
                                stat2[:, tg * 4 + 2:tg * 4 + 3],
                                stat2[:, tg * 4 + 3:tg * 4 + 4],
                                ALU.add, ALU.mult, wait=w, inc=sE)
                ph.op("y", "dma_start", AP(out, tg * 128 * D, [[D, 128], [1, D]]),
                      AP(o2[tg % 2], 0, [[D, 128], [1, D]]),
                      wait=[(sE, odn[tg])], inc=sO, incn=16)
                outd[tg] = sO.n
            ph.wait("y", sO, sO.n)
            ph.run()

    return nc


_NC_CACHE = [None]


def _tri():
    tri = np.zeros((128, GK, 512), np.float32)
    ii = np.arange(128)[:, None]
    qq = np.arange(512)[None, :]
    for j in range(GK):
        tri[:, j, :] = (qq >= j * 128 + ii).astype(np.float32)
    return np.ascontiguousarray(tri.reshape(128, GK * 512)).astype(BF)


def _rearr_q(rows):
    return np.ascontiguousarray(rows.T.reshape(C, 128, 512).transpose(1, 0, 2))


def kernel(L0_attn_x, L0_attn_bv_q, L0_attn_bv_k, L0_attn_bv_v,
           L0_ffn_W, L0_ffn_b, L0_ffn_A, L0_ffn_B,
           L0_ffn_router_W, L0_ffn_router_b):
    X = np.ascontiguousarray(np.asarray(L0_attn_x, np.float32).reshape(T, D))
    XT = np.ascontiguousarray(X.T)
    WT = np.ascontiguousarray(np.asarray(L0_ffn_W, np.float32).T)
    AT = np.asarray(L0_ffn_A, np.float32).reshape(128, D).T
    ATS = np.ascontiguousarray(AT.reshape(C, 128, 128).transpose(1, 0, 2))
    BM = np.ascontiguousarray(
        np.asarray(L0_ffn_B, np.float32).transpose(0, 2, 1).reshape(128, D))
    RWT = np.asarray(L0_ffn_router_W, np.float32).T
    RWTS = np.ascontiguousarray(RWT.reshape(C, 128, 8).transpose(1, 0, 2))
    rb = np.ascontiguousarray(np.asarray(L0_ffn_router_b, np.float32).reshape(1, 8))
    bias = np.ascontiguousarray(np.asarray(L0_ffn_b, np.float32).reshape(1, D))
    bvqa = np.ascontiguousarray(np.asarray(L0_attn_bv_q, np.float32))
    bvka = np.ascontiguousarray(np.asarray(L0_attn_bv_k, np.float32))
    bvva = np.ascontiguousarray(np.asarray(L0_attn_bv_v, np.float32))

    shared = dict(
        x=X, xt=XT, wt=WT, ats=ATS, bm=BM, rwts=RWTS, rbs=rb, biasr=bias,
        bvq=bvqa, bvk=bvka, bvv=bvva,
        identb=np.eye(128, dtype=BF),
        identf=np.eye(128, dtype=np.float32),
        ones1f=np.ones((1, 128), np.float32),
        ones1b=np.ones((1, 128), BF),
        rep8=np.repeat(np.eye(8, dtype=np.float32), 16, axis=1),
        epsc=np.full((128, 1), EPS, np.float32),
        trim=_tri(),
    )
    in_maps = []
    for c in range(NCORE):
        B0 = 512 * (15 - c)
        A0 = 512 * c
        q = np.stack([_rearr_q(X[B0:B0 + 512]), _rearr_q(X[A0:A0 + 512])])
        xrh = np.stack([X[B0:B0 + 512], X[A0:A0 + 512]])
        loc = np.concatenate([np.arange(B0, B0 + 512), np.arange(A0, A0 + 512)])
        mqv = ((loc >= Q0) & (loc < Q1)).astype(np.float32)
        mqs = np.ascontiguousarray(mqv.reshape(8, 128).T).astype(BF)
        m = dict(shared)
        m.update(qsrc=q, xr=np.ascontiguousarray(xrh), mq=mqs)
        in_maps.append(m)

    if _NC_CACHE[0] is None:
        _NC_CACHE[0] = build_program()
    nc = _NC_CACHE[0]
    res = run_bass_kernel_spmd(nc, in_maps, core_ids=list(range(NCORE)))

    full = np.empty((1, T, D), np.float32)
    for c in range(NCORE):
        r = res.results[c]["out"]
        B0 = 512 * (15 - c)
        A0 = 512 * c
        full[0, B0:B0 + 512] = r[0:512]
        full[0, A0:A0 + 512] = r[512:1024]
    return full


# version 13
# speedup vs baseline: 1.1708x; 1.0084x over previous
import contextlib
import numpy as np
import ml_dtypes
import concourse.bass as bass
import concourse.mybir as mybir
from concourse.bass_utils import run_bass_kernel_spmd

F32 = mybir.dt.float32
BF16 = mybir.dt.bfloat16
AF = mybir.ActivationFunctionType
ALU = mybir.AluOpType
AX = mybir.AxisListType.X
BF = ml_dtypes.bfloat16

T, D, H, HD = 8192, 4096, 4, 1024
C = 32
NG1, NG2 = 16, 8
GK = 4
NCORE = 8
EPS = 1e-5
Q0, Q1 = 611, 8191
NVALID = Q1 - Q0
SCALING = 2.0


def colperm(c):
    return (c % 8) * 4 + (c // 8)


def invperm(cc):
    return 8 * (cc % 4) + cc // 4


class Sem:
    def __init__(self, es, nc, name):
        self.h = es.enter_context(nc.semaphore(name))
        self.n = 0


class Phase:

    def __init__(self, nc):
        self.nc = nc
        self.q = {k: [] for k in "tvsgy"}

    def op(self, k, method, *args, wait=(), inc=None, incn=1, **kwargs):
        tgt = None
        if inc is not None:
            inc.n += incn
            tgt = inc.n
        waits = [(s.h, v) for (s, v) in wait]

        def f(eng, method=method, args=args, kwargs=kwargs, waits=waits,
              inc=inc, incn=incn):
            for h, v in waits:
                eng.wait_ge(h, v)
            ins = getattr(eng, method)(*args, **kwargs)
            if inc is not None:
                ins.then_inc(inc.h, incn)
        self.q[k].append(f)
        return tgt

    def wait(self, k, sem, val):
        self.q[k].append(lambda eng, h=sem.h, v=val: eng.wait_ge(h, v))

    def run(self):
        with self.nc.Block() as block:
            for key, starter in (("g", block.gpsimd), ("y", block.sync),
                                 ("t", block.tensor), ("s", block.scalar),
                                 ("v", block.vector)):
                fns = self.q[key]
                if fns:
                    starter(lambda eng, fns=fns: [f(eng) for f in fns])

    def run_flat(self, engmap):
        for key in "gytsv":
            eng = engmap[key]
            for f in self.q[key]:
                f(eng)


def build_program():
    nc = bass.Bass("TRN2", target_bir_lowering=False, num_devices=NCORE)
    AP = bass.AP

    x = nc.dram_tensor("x", [T, D], F32, kind="ExternalInput")
    xt = nc.dram_tensor("xt", [D, T], F32, kind="ExternalInput")
    qsrc = nc.dram_tensor("qsrc", [2, 128, C, 512], F32, kind="ExternalInput")
    xr = nc.dram_tensor("xr", [2, 512, D], F32, kind="ExternalInput")
    mq = nc.dram_tensor("mq", [128, 8], BF16, kind="ExternalInput")
    trim = nc.dram_tensor("trim", [128, GK * 512], BF16, kind="ExternalInput")
    wt = nc.dram_tensor("wt", [D, D], F32, kind="ExternalInput")
    ats = nc.dram_tensor("ats", [128, C, 128], F32, kind="ExternalInput")
    bm = nc.dram_tensor("bm", [128, D], F32, kind="ExternalInput")
    rwts = nc.dram_tensor("rwts", [128, C, 8], F32, kind="ExternalInput")
    rbs = nc.dram_tensor("rbs", [1, 8], F32, kind="ExternalInput")
    biasr = nc.dram_tensor("biasr", [1, D], F32, kind="ExternalInput")
    bvq = nc.dram_tensor("bvq", [4, HD], F32, kind="ExternalInput")
    bvk = nc.dram_tensor("bvk", [4, HD], F32, kind="ExternalInput")
    bvv = nc.dram_tensor("bvv", [4, HD], F32, kind="ExternalInput")
    identb = nc.dram_tensor("identb", [128, 128], BF16, kind="ExternalInput")
    identf = nc.dram_tensor("identf", [128, 128], F32, kind="ExternalInput")
    ones1f = nc.dram_tensor("ones1f", [1, 128], F32, kind="ExternalInput")
    ones1b = nc.dram_tensor("ones1b", [1, 128], BF16, kind="ExternalInput")
    rep8 = nc.dram_tensor("rep8", [8, 128], F32, kind="ExternalInput")
    epsc = nc.dram_tensor("epsc", [128, 1], F32, kind="ExternalInput")

    out = nc.dram_tensor("out", [1024, D], F32, kind="ExternalOutput")

    y1sp = nc.dram_tensor("y1sp", [8, 128, D], F32, kind="Internal")
    y1td = nc.dram_tensor("y1td", [C, 128, 1024], BF16, kind="Internal")
    spd = nc.dram_tensor("spd", [8, 128, D], F32, kind="Internal")
    qsb_d = nc.dram_tensor("qsb_d", [1, D], F32, kind="Internal")
    qsa_d = nc.dram_tensor("qsa_d", [1, D], F32, kind="Internal")
    vvrow_d = nc.dram_tensor("vvrow_d", [C, 128], F32, kind="Internal")
    ccin = nc.dram_tensor("ccin", [1, D], F32, kind="Internal")
    ccout = nc.dram_tensor("ccout", [1, D], F32, kind="Internal")

    ES = contextlib.ExitStack()
    with ES:
        sbp = lambda name, shape, dt: ES.enter_context(nc.sbuf_tensor(name, shape, dt))
        identb_s = sbp("identb_s", [128, 128], BF16)
        identf_s = sbp("identf_s", [128, 128], F32)
        ones1f_s = sbp("ones1f_s", [1, 128], F32)
        ones1b_s = sbp("ones1b_s", [1, 128], BF16)
        rep8_s = sbp("rep8_s", [8, 128], F32)
        mq_s = sbp("mq_s", [128, 8], BF16)
        qkv32 = sbp("qkv32", [128, C], F32)
        stat1 = sbp("stat1", [128, 32], F32)
        stat2 = sbp("stat2", [128, 32], F32)
        svar = sbp("svar", [128, 2], F32)
        rout128 = sbp("rout128", [128, 1], F32)
        rbs_s = sbp("rbs_s", [1, 8], F32)
        epsc_s = sbp("epsc_s", [128, 1], F32)
        tri_s = sbp("tri_s", [128, GK * 512], BF16)

        sK = Sem(ES, nc, "sK")
        sV = Sem(ES, nc, "sVd")
        sG = Sem(ES, nc, "sG")
        sY = Sem(ES, nc, "sY")
        sT = Sem(ES, nc, "sT")
        sS = Sem(ES, nc, "sS")
        sE = Sem(ES, nc, "sE")
        sCC = Sem(ES, nc, "sCC")
        sX = Sem(ES, nc, "sX")
        sSp = Sem(ES, nc, "sSp")
        sTs = Sem(ES, nc, "sTs")
        sQt = Sem(ES, nc, "sQt")
        sW0 = Sem(ES, nc, "sW0")
        sW1 = Sem(ES, nc, "sW1")
        sSd = Sem(ES, nc, "sSd")
        sR0 = Sem(ES, nc, "sR0")
        sR1 = Sem(ES, nc, "sR1")
        sO = Sem(ES, nc, "sO")
        sBar = Sem(ES, nc, "sBar")
        sG2 = Sem(ES, nc, "sG2")

        def vop(ph, method, *args, wait=(), **kw):
            v = ph.op("v", method, *args, wait=wait, inc=sE, **kw)
            ph.wait("v", sE, v)
            return v

        def tpose(ph, out_ap, in_ap, id_ap, wait=()):
            v = ph.op("t", "transpose", out_ap, in_ap, id_ap, wait=wait, inc=sT)
            ph.wait("t", sT, v)
            return v

        ph = Phase(nc)
        for dst, src in ((identb_s, identb), (identf_s, identf), (ones1f_s, ones1f),
                         (ones1b_s, ones1b), (rep8_s, rep8), (mq_s, mq), (rbs_s, rbs),
                         (epsc_s, epsc), (tri_s, trim)):
            ph.op("y", "dma_start", dst[:, :], src.ap(), inc=sY, incn=16)
        ph.wait("y", sY, sY.n)
        ph.run()

        ph = Phase(nc)
        with contextlib.ExitStack() as E1:
            sb1 = lambda name, shape, dt: E1.enter_context(nc.sbuf_tensor(name, shape, dt))
            bv = sb1("bv", [4, 3 * HD], F32)
            hard = sb1("hard", [4, 3 * HD], F32)
            alpha = sb1("alpha", [4, 3], F32)
            tscr = sb1("tscr", [4, HD], F32)
            qkvhm = sb1("qkvhm", [4, HD], F32)
            vv32 = sb1("vv32", [128, C], F32)
            vvT32 = sb1("vvT32", [C, 128], F32)
            ps_b = E1.enter_context(nc.psum_tensor("ps_b", [128, C], F32))
            ps_c = E1.enter_context(nc.psum_tensor("ps_c", [C, 128], F32))

            for i, src in enumerate((bvq, bvk, bvv)):
                ph.op("y", "dma_start", bv[:, i * HD:(i + 1) * HD], src.ap(),
                      inc=sY, incn=16)
            wld = sY.n
            for i in range(3):
                sl = bv[:, i * HD:(i + 1) * HD]
                hsl = hard[:, i * HD:(i + 1) * HD]
                vop(ph, "reduce_sum", alpha[:, i:i + 1], sl, AX,
                    apply_absolute_value=True, wait=[(sY, wld)])
                vop(ph, "tensor_scalar", alpha[:, i:i + 1], alpha[:, i:i + 1],
                    1.0 / HD, None, ALU.mult)
                wsg = ph.op("s", "activation", hsl, sl, AF.Sign,
                            wait=[(sY, wld)], inc=sS)
                vop(ph, "tensor_tensor", tscr[:, :], hsl, hsl, ALU.mult,
                    wait=[(sS, wsg)])
                vop(ph, "tensor_scalar", tscr[:, :], tscr[:, :], -1.0, 1.0,
                    ALU.mult, ALU.add)
                vop(ph, "tensor_tensor", hsl, hsl, tscr[:, :], ALU.add)
                vop(ph, "tensor_scalar", hsl, hsl, alpha[:, i:i + 1], None, ALU.mult)
            wqk = vop(ph, "tensor_tensor", qkvhm[:, :], hard[:, 0:HD],
                      hard[:, HD:2 * HD], ALU.mult)
            for j in range(8):
                w1 = tpose(ph, ps_b[:, 4 * j:4 * j + 4],
                           qkvhm[:, j * 128:(j + 1) * 128], identf_s[0:4, 0:4],
                           wait=[(sE, wqk)] if j == 0 else ())
            wcp1 = ph.op("v", "tensor_copy", qkv32[:, :], ps_b[:, :],
                         wait=[(sT, w1)], inc=sE)
            for j in range(8):
                w2 = tpose(ph, ps_b[:, 4 * j:4 * j + 4],
                           hard[:, 2 * HD + j * 128:2 * HD + (j + 1) * 128],
                           identf_s[0:4, 0:4],
                           wait=[(sE, wcp1)] if j == 0 else ())
            wcp2 = ph.op("v", "tensor_copy", vv32[:, :], ps_b[:, :],
                         wait=[(sT, w2)], inc=sE)
            w3 = tpose(ph, ps_c[:, :], vv32[:, :], identf_s[:, :],
                       wait=[(sE, wcp2)])
            wvt = ph.op("v", "tensor_copy", vvT32[:, :], ps_c[:, :],
                        wait=[(sT, w3)], inc=sE)
            ph.op("y", "dma_start", vvrow_d.ap(), vvT32[:, :],
                  wait=[(sE, wvt)], inc=sY, incn=16)
            ph.wait("y", sY, sY.n)
            ph.run()

        with contextlib.ExitStack() as EA:
            acc = EA.enter_context(nc.sbuf_tensor("acc", [128, 4 * D], F32))

            def sweep(sweep_i, NG, diag, body):
                ph = Phase(nc)
                with contextlib.ExitStack() as E2:
                    sb2 = lambda n, sh, dt: E2.enter_context(
                        nc.sbuf_tensor(f"{n}_{sweep_i}_{body}", sh, dt))
                    Qs = sb2("Qs", [128, C * 512], BF16)
                    wqs = None
                    with nc.sbuf_tensor(f"Qraw_{sweep_i}_{body}", [128, C * 512], F32) as Qraw:
                        wqr = ph.op("g", "dma_start",
                                    AP(Qraw, 0, [[C * 512, 128], [1, C * 512]]),
                                    AP(qsrc, sweep_i * 128 * C * 512,
                                       [[C * 512, 128], [1, C * 512]]),
                                    inc=sG, incn=16)
                        for c in range(C):
                            wqs = ph.op("v", "tensor_scalar",
                                        Qs[:, c * 512:(c + 1) * 512],
                                        Qraw[:, c * 512:(c + 1) * 512],
                                        qkv32[:, colperm(c):colperm(c) + 1], None,
                                        ALU.mult,
                                        wait=[(sG, wqr)] if c == 0 else (), inc=sE)
                    Kg = sb2("Kg", [128, C * 512], BF16)
                    Vg = sb2("Vg", [128, GK * D], BF16)
                    Pt = sb2("Pt", [128, 16 * 512], BF16)
                    ps_sc = [E2.enter_context(
                        nc.psum_tensor(f"ps_sc{sweep_i}_{body}_{i}", [128, 512], F32))
                             for i in range(2)]
                    ps_o = [E2.enter_context(
                        nc.psum_tensor(f"ps_o{sweep_i}_{body}_{i}", [128, 512], F32))
                            for i in range(4)]

                    wmz = ph.op("v", "memset", acc[:, :], 0.0, inc=sE)
                    ph.wait("v", sE, wmz)

                    kdone, vdone = {}, {}
                    sigval, mm1val, maskval, chval, addval = {}, {}, {}, {}, {}
                    ph1done, ph2done = {}, {}

                    def emit_dma(g):
                        w = [(sE, wqs)] if g == 0 else [(sT, ph1done[g - 1])]
                        for cs in range(8):
                            ph.op("g", "dma_start",
                                  AP(Kg, cs * 4 * 512,
                                     [[C * 512, 128], [512, 4], [1, 512]]),
                                  AP(xt, (cs * 4 * 128) * T + 512 * g,
                                     [[T, 128], [128 * T, 4], [1, 512]]),
                                  wait=w if cs == 0 else (), inc=sK, incn=16)
                        kdone[g] = sK.n
                        wv = [(sE, wqs)] if g == 0 else [(sT, ph2done[g - 1])]
                        ph.op("g", "dma_start",
                              AP(Vg, 0, [[GK * D, 128], [D, GK], [1, D]]),
                              AP(x, 512 * g * D, [[D, 128], [128 * D, GK], [1, D]]),
                              wait=wv, inc=sV, incn=16)
                        vdone[g] = sV.n

                    emit_dma(0)
                    ci_glob = 0
                    for g in range(NG):
                        for i in range(16):
                            j, h = i // 4, i % 4
                            gi = g * 16 + i
                            bank = ps_sc[gi % 2]
                            for cc in range(8):
                                c = 8 * h + cc
                                w = []
                                if cc == 0:
                                    if i == 0:
                                        w.append((sK, kdone[g]))
                                        if g == 0:
                                            w.append((sE, wqs))
                                    if gi >= 2:
                                        w.append((sS, sigval[gi - 2]))
                                ph.op("t", "matmul", bank[:, :],
                                      Kg[:, c * 512 + j * 128:c * 512 + j * 128 + 128],
                                      Qs[:, c * 512:(c + 1) * 512],
                                      start=(cc == 0), stop=(cc == 7),
                                      wait=w, inc=sT if cc == 7 else None)
                            mm1val[gi] = sT.n
                            w = [(sT, mm1val[gi])]
                            if i == 0 and g > 0:
                                w.append((sT, ph2done[g - 1]))
                            sigval[gi] = ph.op("s", "activation",
                                               Pt[:, i * 512:(i + 1) * 512],
                                               bank[:, :], AF.Sigmoid,
                                               scale=0.125, wait=w, inc=sS)
                            if g == diag:
                                maskval[(g, i)] = ph.op(
                                    "v", "tensor_tensor", Pt[:, i * 512:(i + 1) * 512],
                                    Pt[:, i * 512:(i + 1) * 512],
                                    tri_s[:, j * 512:(j + 1) * 512], ALU.mult,
                                    wait=[(sS, sigval[gi])], inc=sE)
                        ph1done[g] = sT.n
                        first_chain = True
                        for h in range(H):
                            for qs in range(4):
                                for dh in range(2):
                                    bank = ps_o[ci_glob % 4]
                                    for j in range(GK):
                                        w = []
                                        if j == 0:
                                            if first_chain:
                                                w.append((sV, vdone[g]))
                                                if g == diag:
                                                    w.append((sE, maskval[(g, 15)]))
                                                else:
                                                    w.append((sS, sigval[g * 16 + 15]))
                                            if ci_glob >= 4:
                                                w.append((sE, addval[ci_glob - 4]))
                                        ph.op("t", "matmul", bank[:, :],
                                              Pt[:, (4 * j + h) * 512 + qs * 128:
                                                 (4 * j + h) * 512 + qs * 128 + 128],
                                              Vg[:, j * D + h * HD + dh * 512:
                                                 j * D + h * HD + dh * 512 + 512],
                                              start=(j == 0), stop=(j == GK - 1),
                                              wait=w, inc=sT if j == GK - 1 else None)
                                    chval[ci_glob] = sT.n
                                    first_chain = False
                                    wadd = [(sT, chval[ci_glob])]
                                    if ci_glob >= 32:
                                        wadd.append((sE, addval[ci_glob - 32]))
                                    addval[ci_glob] = ph.op(
                                        "v", "tensor_tensor",
                                        acc[:, qs * D + h * HD + dh * 512:
                                            qs * D + h * HD + dh * 512 + 512],
                                        acc[:, qs * D + h * HD + dh * 512:
                                            qs * D + h * HD + dh * 512 + 512],
                                        bank[:, :], ALU.add,
                                        wait=wadd, inc=sE)
                                    ci_glob += 1
                        ph2done[g] = sT.n
                        if g + 1 < NG:
                            emit_dma(g + 1)
                    ph.run_flat(ENG)

            def ln1(p, body):
                ph = Phase(nc)
                with contextlib.ExitStack() as E3:
                    sb3 = lambda n, sh, dt: E3.enter_context(
                        nc.sbuf_tensor(f"{n}_{p}_{body}", sh, dt))
                    vvrow = sb3("vvrow", [1, D], F32)
                    vvb = sb3("vvb", [128, D], F32)
                    xrb = sb3("xrb", [128, D], F32)
                    s1 = sb3("s1", [128, D], F32)
                    y1f = sb3("y1f", [128, D], F32)
                    y1b4 = sb3("y1b4", [128, 4 * D], BF16)
                    y1ts = [sb3(f"y1ts{i}", [128, 1024], BF16) for i in range(2)]
                    junk = sb3("junk", [128, D], BF16)
                    qtmp = sb3("qtmp", [1, 1024], F32)
                    ps_tp = [E3.enter_context(
                        nc.psum_tensor(f"ps_tp{p}_{body}_{i}", [128, 1024], BF16))
                             for i in range(2)]
                    ps_vb = [E3.enter_context(
                        nc.psum_tensor(f"ps_vb{p}_{body}_{i}", [128, 512], F32))
                             for i in range(2)]
                    ps_q = [E3.enter_context(
                        nc.psum_tensor(f"ps_q{p}_{body}_{i}", [1, 512], F32))
                            for i in range(2)]

                    wvr0 = ph.op("y", "dma_start", vvrow[:, :],
                                 bass.AP(vvrow_d, 0, [[C * 128, 1], [1, C * 128]]),
                                 inc=sY, incn=16)
                    vbmm, vbev = {}, {}
                    for grp in range(8):
                        bank = ps_vb[grp % 2]
                        for ci in range(4):
                            cc = grp * 4 + ci
                            w = []
                            if ci == 0 and grp == 0:
                                w.append((sY, wvr0))
                            if ci == 0 and grp >= 2:
                                w.append((sE, vbev[grp - 2]))
                            vbmm[grp] = ph.op(
                                "t", "matmul", bank[:, ci * 128:(ci + 1) * 128],
                                ones1f_s[:, :], vvrow[0:1, cc * 128:(cc + 1) * 128],
                                start=True, stop=True, wait=w, inc=sT)
                            ph.wait("t", sT, vbmm[grp])
                        for ci in range(4):
                            cc = grp * 4 + ci
                            c = invperm(cc)
                            vbev[grp] = ph.op("v", "tensor_copy",
                                              vvb[:, c * 128:(c + 1) * 128],
                                              bank[:, ci * 128:(ci + 1) * 128],
                                              wait=[(sT, vbmm[grp])] if ci == 0 else (),
                                              inc=sE)
                    xrd, s1dn, y1fd, y1bd = {}, {}, {}, {}
                    spl = {}
                    for k in range(4):
                        tg = p * 4 + k
                        w = [] if k == 0 else [(sE, s1dn[k - 1])]
                        ph.op("y", "dma_start",
                              AP(xrb, 0, [[D, 128], [1, D]]),
                              AP(xr, (p * 512 + k * 128) * D, [[D, 128], [1, D]]),
                              wait=w, inc=sX, incn=16)
                        xrd[k] = sX.n
                        w = [(sE, vbev[7])] if k == 0 else [(sE, y1fd[k - 1])]
                        wtt1 = ph.op("v", "tensor_tensor", s1[:, :],
                                     acc[:, k * D:(k + 1) * D], vvb[:, :], ALU.mult,
                                     wait=w, inc=sE)
                        ph.wait("v", sE, wtt1)
                        s1dn[k] = ph.op("v", "scalar_tensor_tensor", s1[:, :],
                                        s1[:, :], 1.0, xrb[:, :], ALU.mult, ALU.add,
                                        accum_out=stat1[:, tg * 4:tg * 4 + 1],
                                        wait=[(sX, xrd[k])], inc=sE)
                        wsqw = [(sE, s1dn[k])]
                        if k >= 1:
                            wsqw.append((sS, y1bd[k - 1]))
                        wsq = ph.op("s", "activation", junk[:, :], s1[:, :], AF.Square,
                                    accum_out=stat1[:, tg * 4 + 1:tg * 4 + 2],
                                    wait=wsqw, inc=sS)
                        vop(ph, "tensor_scalar", stat1[:, tg * 4 + 2:tg * 4 + 3],
                            stat1[:, tg * 4:tg * 4 + 1], -1.0 / D, None, ALU.mult,
                            wait=[(sS, wsq)])
                        vop(ph, "tensor_scalar", svar[:, 0:1],
                            stat1[:, tg * 4 + 1:tg * 4 + 2], 1.0 / D, None, ALU.mult)
                        vop(ph, "tensor_tensor", svar[:, 1:2],
                            stat1[:, tg * 4 + 2:tg * 4 + 3],
                            stat1[:, tg * 4 + 2:tg * 4 + 3], ALU.mult)
                        wvr = vop(ph, "tensor_tensor", svar[:, 0:1], svar[:, 0:1],
                                  svar[:, 1:2], ALU.subtract)
                        wsd = ph.op("s", "activation", svar[:, 1:2], svar[:, 0:1],
                                    AF.Sqrt, bias=epsc_s[:, :], wait=[(sE, wvr)], inc=sS)
                        vop(ph, "reciprocal", stat1[:, tg * 4 + 3:tg * 4 + 4],
                            svar[:, 1:2], wait=[(sS, wsd)])
                        w = []
                        if k >= 1:
                            w.append((sSp, spl[k - 1]))
                            w.append((sS, y1bd[k - 1]))
                        y1fd[k] = ph.op("v", "tensor_scalar", y1f[:, :], s1[:, :],
                                        stat1[:, tg * 4 + 2:tg * 4 + 3],
                                        stat1[:, tg * 4 + 3:tg * 4 + 4],
                                        ALU.add, ALU.mult, wait=w, inc=sE)
                        ph.op("y", "dma_start",
                              AP(y1sp, tg * 128 * D, [[D, 128], [1, D]]),
                              AP(y1f, 0, [[D, 128], [1, D]]),
                              wait=[(sE, y1fd[k])], inc=sSp, incn=16)
                        spl[k] = sSp.n
                        y1bd[k] = ph.op("s", "copy", y1b4[:, k * D:(k + 1) * D],
                                        y1f[:, :],
                                        wait=[(sE, y1fd[k])], inc=sS)
                    tpmm, tpev, tsd = {}, {}, {}
                    for cg in range(16):
                        bank = ps_tp[cg % 2]
                        for ci in range(2):
                            c = cg * 2 + ci
                            for tk in range(4):
                                w = []
                                if ci == 0 and tk == 0:
                                    if cg == 0:
                                        w.append((sS, y1bd[3]))
                                    if cg >= 2:
                                        w.append((sE, tpev[cg - 2]))
                                tpmm[cg] = tpose(
                                    ph,
                                    bank[:, (ci * 4 + tk) * 128:(ci * 4 + tk + 1) * 128],
                                    y1b4[:, tk * D + c * 128:tk * D + (c + 1) * 128],
                                    identb_s[:, :], wait=w)
                        w = [(sT, tpmm[cg])]
                        if cg >= 1:
                            w.append((sTs, tsd[cg - 1]))
                        tpev[cg] = ph.op("v", "tensor_copy", y1ts[cg % 2][:, :],
                                         bank[:, :], wait=w, inc=sE)
                        ph.op("y", "dma_start",
                              AP(y1td, (cg * 2) * 128 * 1024 + p * 512,
                                 [[1024, 128], [128 * 1024, 2], [1, 512]]),
                              AP(y1ts[cg % 2], 0, [[1024, 128], [512, 2], [1, 512]]),
                              wait=[(sE, tpev[cg])], inc=sTs, incn=16)
                        tsd[cg] = sTs.n
                    qmm, qev, qdm = {}, {}, {}
                    for fc in range(8):
                        bank = ps_q[fc % 2]
                        for tk in range(4):
                            w = []
                            if tk == 0 and fc >= 2:
                                w.append((sE, qev[fc - 2]))
                            ph.op("t", "matmul", bank[:, :],
                                  mq_s[:, p * 4 + tk:p * 4 + tk + 1],
                                  y1b4[:, tk * D + fc * 512:tk * D + (fc + 1) * 512],
                                  start=(tk == 0), stop=(tk == 3),
                                  wait=w, inc=sT if tk == 3 else None)
                        qmm[fc] = sT.n
                        w = [(sT, qmm[fc])]
                        if fc >= 1:
                            w.append((sQt, qdm[fc - 1]))
                        qev[fc] = ph.op("v", "tensor_copy",
                                        qtmp[0:1, (fc % 2) * 512:(fc % 2) * 512 + 512],
                                        bank[0:1, :], wait=w, inc=sE)
                        qd = qsb_d if p == 0 else qsa_d
                        ph.op("y", "dma_start",
                              qd[0:1, fc * 512:(fc + 1) * 512],
                              qtmp[0:1, (fc % 2) * 512:(fc % 2) * 512 + 512],
                              wait=[(sE, qev[fc])], inc=sQt, incn=16)
                        qdm[fc] = sQt.n
                    ph.wait("y", sY, sY.n)
                    ph.wait("y", sX, sX.n)
                    ph.wait("y", sSp, sSp.n)
                    ph.wait("y", sTs, sTs.n)
                    ph.wait("y", sQt, sQt.n)
                    ph.run()

            engs = [nc.tensor, nc.vector, nc.scalar, nc.gpsimd, nc.sync]
            ENG = {"t": nc.tensor, "v": nc.vector, "s": nc.scalar,
                   "g": nc.gpsimd, "y": nc.sync}
            body_sems = [sK, sV, sG, sY, sX, sSp, sTs, sQt, sT, sS, sE]

            carry = {b: {} for b in range(8)}

            def swept(sweep_i):
                NGmax = NG1 if sweep_i == 0 else NG2
                PAD = {id(sT): 48 * NGmax, id(sS): 16 * NGmax,
                       id(sE): 1 + 32 + 16 + 32 * NGmax}
                dma_sems = [sK, sV, sG]
                pad_sems = [sT, sS, sE]
                snap = [(s, s.n) for s in body_sems]
                sbar0 = sBar.n
                idx = [e.partition_id() for e in engs]
                for body in nc.Switch(engines=engs, index=idx, n=8):
                    for s, v in snap:
                        s.n = v
                    for s in dma_sems:
                        if id(s) in carry[body]:
                            s.n = carry[body][id(s)]
                    sBar.n = sbar0
                    if sweep_i == 0:
                        sweep(0, 16 - body, 15 - body, body)
                    else:
                        sweep(1, body + 1, body, body)
                    for s, _ in snap:
                        nc.gpsimd.wait_ge(s.h, s.n)
                    for s, v0 in snap:
                        if id(s) in PAD:
                            assert s.n - v0 <= PAD[id(s)], (s.h.name, s.n - v0)
                            pad = v0 + PAD[id(s)] - s.n
                            if pad:
                                nc.gpsimd.sem_inc(s.h, pad)
                    for s in dma_sems:
                        carry[body][id(s)] = s.n
                    tgt = sBar.n + 5
                    for e in engs:
                        e.nop().then_inc(sBar.h, 1)
                    for e in engs:
                        e.wait_ge(sBar.h, tgt)
                    sBar.n = tgt
                for s, v0 in snap:
                    s.n = v0 + PAD[id(s)] if id(s) in PAD else v0

            swept(0)
            ln1(0, 0)
            swept(1)
            ln1(1, 0)

        with contextlib.ExitStack() as EF:
            sbf = lambda n, sh, dt: EF.enter_context(nc.sbuf_tensor(n, sh, dt))
            y1T = sbf("y1T", [128, C * 1024], BF16)
            Abf = sbf("Abf", [128, C * 128], BF16)
            Bmbf = sbf("Bmbf", [128, D], BF16)
            rwtsb = sbf("rwtsb", [128, C * 8], BF16)
            bb16 = sbf("bb16", [1, D], BF16)
            tTs = sbf("tTs", [128, 1024], BF16)

            ph = Phase(nc)
            with contextlib.ExitStack() as E6:
                sb6 = lambda n, sh, dt: E6.enter_context(nc.sbuf_tensor(n, sh, dt))
                qrow = sb6("qrow", [1, D], F32)
                qrow2 = sb6("qrow2", [1, D], F32)
                qmf = sb6("qmf", [128, C], F32)
                qmfb = sb6("qmfb", [128, C], BF16)
                lg = sb6("lg", [1, 8], F32)
                mx = sb6("mx", [1, 1], F32)
                ex = sb6("ex", [1, 8], F32)
                sm = sb6("sm", [1, 1], F32)
                rt = sb6("rt", [1, 8], F32)
                r8 = sb6("r8", [8, 1], F32)
                ps_r = E6.enter_context(nc.psum_tensor("ps_r", [1, 8], F32))
                ps_q2 = E6.enter_context(nc.psum_tensor("ps_q2", [128, C], F32))
                ps_r8 = E6.enter_context(nc.psum_tensor("ps_r8", [8, 1], F32))
                ps_r128 = E6.enter_context(nc.psum_tensor("ps_r128", [128, 1], F32))

                for cs in range(8):
                    ph.op("g", "dma_start",
                          AP(y1T, cs * 4 * 1024, [[C * 1024, 128], [1024, 4], [1, 1024]]),
                          AP(y1td, cs * 4 * 128 * 1024,
                             [[1024, 128], [128 * 1024, 4], [1, 1024]]),
                          inc=sG2, incn=16)
                ph.op("g", "dma_start", Abf[:, :],
                      AP(ats, 0, [[C * 128, 128], [1, C * 128]]), inc=sG2, incn=16)
                ph.op("g", "dma_start", Bmbf[:, :], bm.ap(), inc=sG2, incn=16)
                ph.op("g", "dma_start", rwtsb[:, :],
                      AP(rwts, 0, [[C * 8, 128], [1, C * 8]]), inc=sG2, incn=16)
                ph.op("g", "dma_start", bb16[:, :], biasr.ap(), inc=sG2, incn=16)
                wffl = sG2.n

                ph.op("y", "dma_start", qrow[:, :], qsb_d.ap(), inc=sY, incn=16)
                ph.op("y", "dma_start", qrow2[:, :], qsa_d.ap(), inc=sY, incn=16)
                wq2 = sY.n
                wqa = ph.op("v", "tensor_tensor", qrow[:, :], qrow[:, :], qrow2[:, :],
                            ALU.add, wait=[(sY, wq2)], inc=sE)
                wcci = ph.op("y", "dma_start", ccin.ap(), qrow[:, :],
                             wait=[(sE, wqa)], inc=sY, incn=16)
                ph.op("g", "collective_compute", "AllReduce", ALU.add,
                      replica_groups=[list(range(NCORE))],
                      ins=[ccin.ap().opt()], outs=[ccout.ap().opt()],
                      wait=[(sY, wcci)], inc=sCC, incn=1)
                wccr = ph.op("y", "dma_start", qrow[:, :], ccout.ap(),
                             wait=[(sCC, 1)], inc=sY, incn=16)
                wqm = ph.op("v", "tensor_scalar", qrow[:, :], qrow[:, :],
                            1.0 / NVALID, None, ALU.mult,
                            wait=[(sY, wccr)], inc=sE)
                for j in range(C):
                    wq2t = tpose(ph, ps_q2[:, j:j + 1],
                                 qrow[0:1, j * 128:(j + 1) * 128], identf_s[0:1, 0:1],
                                 wait=[(sE, wqm)] if j == 0 else ())
                vop(ph, "tensor_copy", qmf[:, :], ps_q2[:, :], wait=[(sT, wq2t)])
                wqb = vop(ph, "tensor_copy", qmfb[:, :], qmf[:, :])
                for c in range(C):
                    w = []
                    if c == 0:
                        w = [(sE, wqb), (sG2, wffl)]
                    ph.op("t", "matmul", ps_r[:, :], qmfb[:, c:c + 1],
                          rwtsb[:, c * 8:(c + 1) * 8],
                          start=(c == 0), stop=(c == C - 1),
                          wait=w, inc=sT if c == C - 1 else None)
                wlg = sT.n
                vop(ph, "scalar_tensor_tensor", lg[:, :], ps_r[:, :], 1.0,
                    rbs_s[:, :], ALU.mult, ALU.add, wait=[(sT, wlg)])
                vop(ph, "reduce_max", mx[:, :], lg[:, :], AX)
                wmx = vop(ph, "tensor_scalar", mx[:, :], mx[:, :], -1.0, None,
                          ALU.mult)
                wex = ph.op("s", "activation", ex[:, :], lg[:, :], AF.Exp,
                            bias=mx[:, :], wait=[(sE, wmx)], inc=sS)
                vop(ph, "reduce_sum", sm[:, :], ex[:, :], AX, wait=[(sS, wex)])
                vop(ph, "reciprocal", sm[:, :], sm[:, :])
                wrt = vop(ph, "tensor_scalar", rt[:, :], ex[:, :], sm[:, :],
                          SCALING, ALU.mult, ALU.mult)
                wr8 = tpose(ph, ps_r8[:, :], rt[0:1, :],
                            identf_s[0:1, 0:1], wait=[(sE, wrt)])
                wr8e = ph.op("v", "tensor_copy", r8[:, :], ps_r8[:, :],
                             wait=[(sT, wr8)], inc=sE)
                wrmm = ph.op("t", "matmul", ps_r128[:, :], rep8_s[:, :], r8[:, :],
                             start=True, stop=True, wait=[(sE, wr8e)], inc=sT)
                wrout = ph.op("v", "tensor_copy", rout128[:, :], ps_r128[:, :],
                              wait=[(sT, wrmm)], inc=sE)
                ph.run()

            ph = Phase(nc)
            with contextlib.ExitStack() as E7:
                ps_tt = [E7.enter_context(nc.psum_tensor(f"ps_tt{i}", [128, 512], F32))
                         for i in range(2)]
                for tb in range(2):
                    for c in range(C):
                        w = [(sG2, wffl)] if (tb == 0 and c == 0) else []
                        ph.op("t", "matmul", ps_tt[tb][:, :],
                              Abf[:, c * 128:(c + 1) * 128],
                              y1T[:, c * 1024 + tb * 512:c * 1024 + tb * 512 + 512],
                              start=(c == 0), stop=(c == C - 1),
                              wait=w, inc=sT if c == C - 1 else None)
                    wtt = sT.n
                    ph.op("v", "tensor_scalar", tTs[:, tb * 512:(tb + 1) * 512],
                          ps_tt[tb][:, :], rout128[:, :], None, ALU.mult,
                          wait=[(sT, wtt), (sE, wrout)], inc=sE)
                wtts = sE.n
                ph.run()

            ph = Phase(nc)
            with contextlib.ExitStack() as E8:
                sb8 = lambda n, sh, dt: E8.enter_context(nc.sbuf_tensor(n, sh, dt))
                Wo = [sb8(f"Wo{i}", [128, C * 512], BF16) for i in range(2)]
                stmp = [sb8(f"stmp{i}", [128, 512], F32) for i in range(4)]
                ps_bs = [E8.enter_context(nc.psum_tensor(f"ps_bs{i}", [128, 512], F32))
                         for i in range(8)]

                wW, bsd, stev, stw = {}, {}, {}, {}

                def emit_w(o):
                    sWp = sW0 if o % 2 == 0 else sW1
                    w = []
                    if o >= 2:
                        w = [(sT, bsd[(o - 2, 7)])]
                    for dg in range(8):
                        ph.op("g", "dma_start",
                              AP(Wo[o % 2], dg * 4 * 512,
                                 [[C * 512, 128], [512, 4], [1, 512]]),
                              AP(wt, (dg * 4 * 128) * D + o * 512,
                                 [[D, 128], [128 * D, 4], [1, 512]]),
                              wait=w if dg == 0 else (), inc=sWp, incn=16)
                    wW[o] = (sWp, sWp.n)

                emit_w(0)
                emit_w(1)
                for o in range(8):
                    for half in range(2):
                        for dg in range(8):
                            for dc in range(4):
                                cch = dg * 4 + dc
                                for tt in range(4):
                                    tglob = half * 4 + tt
                                    w = []
                                    if dg == 0 and dc == 0 and tt == 0:
                                        w.append(wW[o])
                                        if half == 0 and o == 0:
                                            w.append((sE, wtts))
                                        if o >= 1:
                                            w.append((sE, stev[(o - 1, half * 4 + 3)]))
                                    ph.op("t", "matmul", ps_bs[tglob][:, :],
                                          y1T[:, cch * 1024 + tglob * 128:
                                              cch * 1024 + tglob * 128 + 128],
                                          Wo[o % 2][:, cch * 512:(cch + 1) * 512],
                                          start=(dg == 0 and dc == 0), stop=False,
                                          wait=w)
                        for tt in range(4):
                            tglob = half * 4 + tt
                            ph.op("t", "matmul", ps_bs[tglob][:, :],
                                  tTs[:, tglob * 128:(tglob + 1) * 128],
                                  Bmbf[:, o * 512:(o + 1) * 512],
                                  start=False, stop=False)
                            ph.op("t", "matmul", ps_bs[tglob][:, :],
                                  ones1b_s[:, :], bb16[0:1, o * 512:(o + 1) * 512],
                                  start=False, stop=True, inc=sT)
                            bsd[(o, tglob)] = sT.n
                        for tt in range(4):
                            tglob = half * 4 + tt
                            w = [(sT, bsd[(o, tglob)])]
                            ko = (o * 8 + tglob)
                            if ko >= 1:
                                w.append((sSd, stw[ko - 1]))
                            stev[(o, tglob)] = ph.op("v", "tensor_copy",
                                                     stmp[tglob % 4][:, :],
                                                     ps_bs[tglob][:, :], wait=w, inc=sE)
                            ph.op("y", "dma_start",
                                  AP(spd, tglob * 128 * D + o * 512,
                                     [[D, 128], [1, 512]]),
                                  AP(stmp[tglob % 4], 0, [[512, 128], [1, 512]]),
                                  wait=[(sE, stev[(o, tglob)])], inc=sSd, incn=16)
                            stw[ko] = sSd.n
                    if o + 2 < 8:
                        emit_w(o + 2)
                ph.wait("y", sSd, sSd.n)
                ph.run()

        ph = Phase(nc)
        with contextlib.ExitStack() as E9:
            sb9 = lambda n, sh, dt: E9.enter_context(nc.sbuf_tensor(n, sh, dt))
            spb = [sb9(f"spb{i}", [128, D], F32) for i in range(2)]
            y1rb = [sb9(f"y1rb{i}", [128, D], F32) for i in range(2)]
            s2 = sb9("s2", [128, D], F32)
            o2 = [sb9(f"o2{i}", [128, D], F32) for i in range(2)]
            junk2 = sb9("junk2", [128, D], BF16)

            spr, y1r, s2dn, odn, outd = {}, {}, {}, {}, {}
            wsd_prev = [0]
            for tg in range(8):
                sRp = sR0 if tg % 2 == 0 else sR1
                w = []
                if tg >= 2:
                    w = [(sE, s2dn[tg - 2])]
                ph.op("y", "dma_start", AP(spb[tg % 2], 0, [[D, 128], [1, D]]),
                      AP(spd, tg * 128 * D, [[D, 128], [1, D]]),
                      wait=w, inc=sRp, incn=16)
                ph.op("y", "dma_start", AP(y1rb[tg % 2], 0, [[D, 128], [1, D]]),
                      AP(y1sp, tg * 128 * D, [[D, 128], [1, D]]),
                      inc=sRp, incn=16)
                y1r[tg] = sRp.n
                ws2 = [(sRp, y1r[tg])]
                if tg >= 1:
                    ws2.append((sE, odn[tg - 1]))
                s2dn[tg] = ph.op("v", "scalar_tensor_tensor", s2[:, :],
                                 spb[tg % 2][:, :], 1.0, y1rb[tg % 2][:, :],
                                 ALU.mult, ALU.add,
                                 accum_out=stat2[:, tg * 4:tg * 4 + 1],
                                 wait=ws2, inc=sE)
                wsqw = [(sE, s2dn[tg])]
                if tg >= 1:
                    wsqw.append((sS, wsd_prev[0]))
                wsq = ph.op("s", "activation", junk2[:, :], s2[:, :], AF.Square,
                            accum_out=stat2[:, tg * 4 + 1:tg * 4 + 2],
                            wait=wsqw, inc=sS)
                vop(ph, "tensor_scalar", stat2[:, tg * 4 + 2:tg * 4 + 3],
                    stat2[:, tg * 4:tg * 4 + 1], -1.0 / D, None, ALU.mult,
                    wait=[(sS, wsq)])
                vop(ph, "tensor_scalar", svar[:, 0:1],
                    stat2[:, tg * 4 + 1:tg * 4 + 2], 1.0 / D, None, ALU.mult)
                vop(ph, "tensor_tensor", svar[:, 1:2],
                    stat2[:, tg * 4 + 2:tg * 4 + 3],
                    stat2[:, tg * 4 + 2:tg * 4 + 3], ALU.mult)
                wvr = vop(ph, "tensor_tensor", svar[:, 0:1], svar[:, 0:1],
                          svar[:, 1:2], ALU.subtract)
                wsd = ph.op("s", "activation", svar[:, 1:2], svar[:, 0:1], AF.Sqrt,
                            bias=epsc_s[:, :], wait=[(sE, wvr)], inc=sS)
                wsd_prev[0] = wsd
                vop(ph, "reciprocal", stat2[:, tg * 4 + 3:tg * 4 + 4], svar[:, 1:2],
                    wait=[(sS, wsd)])
                w = []
                if tg >= 1:
                    w.append((sO, outd[tg - 1]))
                odn[tg] = ph.op("v", "tensor_scalar", o2[tg % 2][:, :], s2[:, :],
                                stat2[:, tg * 4 + 2:tg * 4 + 3],
                                stat2[:, tg * 4 + 3:tg * 4 + 4],
                                ALU.add, ALU.mult, wait=w, inc=sE)
                ph.op("y", "dma_start", AP(out, tg * 128 * D, [[D, 128], [1, D]]),
                      AP(o2[tg % 2], 0, [[D, 128], [1, D]]),
                      wait=[(sE, odn[tg])], inc=sO, incn=16)
                outd[tg] = sO.n
            ph.wait("y", sO, sO.n)
            ph.run()

    return nc


_NC_CACHE = [None]


def _tri():
    tri = np.zeros((128, GK, 512), np.float32)
    ii = np.arange(128)[:, None]
    qq = np.arange(512)[None, :]
    for j in range(GK):
        tri[:, j, :] = (qq >= j * 128 + ii).astype(np.float32)
    return np.ascontiguousarray(tri.reshape(128, GK * 512)).astype(BF)


def _rearr_q(rows):
    return np.ascontiguousarray(rows.T.reshape(C, 128, 512).transpose(1, 0, 2))


def kernel(L0_attn_x, L0_attn_bv_q, L0_attn_bv_k, L0_attn_bv_v,
           L0_ffn_W, L0_ffn_b, L0_ffn_A, L0_ffn_B,
           L0_ffn_router_W, L0_ffn_router_b):
    X = np.ascontiguousarray(np.asarray(L0_attn_x, np.float32).reshape(T, D))
    XT = np.ascontiguousarray(X.T)
    WT = np.ascontiguousarray(np.asarray(L0_ffn_W, np.float32).T)
    AT = np.asarray(L0_ffn_A, np.float32).reshape(128, D).T
    ATS = np.ascontiguousarray(AT.reshape(C, 128, 128).transpose(1, 0, 2))
    BM = np.ascontiguousarray(
        np.asarray(L0_ffn_B, np.float32).transpose(0, 2, 1).reshape(128, D))
    RWT = np.asarray(L0_ffn_router_W, np.float32).T
    RWTS = np.ascontiguousarray(RWT.reshape(C, 128, 8).transpose(1, 0, 2))
    rb = np.ascontiguousarray(np.asarray(L0_ffn_router_b, np.float32).reshape(1, 8))
    bias = np.ascontiguousarray(np.asarray(L0_ffn_b, np.float32).reshape(1, D))
    bvqa = np.ascontiguousarray(np.asarray(L0_attn_bv_q, np.float32))
    bvka = np.ascontiguousarray(np.asarray(L0_attn_bv_k, np.float32))
    bvva = np.ascontiguousarray(np.asarray(L0_attn_bv_v, np.float32))

    shared = dict(
        x=X, xt=XT, wt=WT, ats=ATS, bm=BM, rwts=RWTS, rbs=rb, biasr=bias,
        bvq=bvqa, bvk=bvka, bvv=bvva,
        identb=np.eye(128, dtype=BF),
        identf=np.eye(128, dtype=np.float32),
        ones1f=np.ones((1, 128), np.float32),
        ones1b=np.ones((1, 128), BF),
        rep8=np.repeat(np.eye(8, dtype=np.float32), 16, axis=1),
        epsc=np.full((128, 1), EPS, np.float32),
        trim=_tri(),
    )
    in_maps = []
    for c in range(NCORE):
        B0 = 512 * (15 - c)
        A0 = 512 * c
        q = np.stack([_rearr_q(X[B0:B0 + 512]), _rearr_q(X[A0:A0 + 512])])
        xrh = np.stack([X[B0:B0 + 512], X[A0:A0 + 512]])
        loc = np.concatenate([np.arange(B0, B0 + 512), np.arange(A0, A0 + 512)])
        mqv = ((loc >= Q0) & (loc < Q1)).astype(np.float32)
        mqs = np.ascontiguousarray(mqv.reshape(8, 128).T).astype(BF)
        m = dict(shared)
        m.update(qsrc=q, xr=np.ascontiguousarray(xrh), mq=mqs)
        in_maps.append(m)

    if _NC_CACHE[0] is None:
        _NC_CACHE[0] = build_program()
    nc = _NC_CACHE[0]
    res = run_bass_kernel_spmd(nc, in_maps, core_ids=list(range(NCORE)))

    full = np.empty((1, T, D), np.float32)
    for c in range(NCORE):
        r = res.results[c]["out"]
        B0 = 512 * (15 - c)
        A0 = 512 * c
        full[0, B0:B0 + 512] = r[0:512]
        full[0, A0:A0 + 512] = r[512:1024]
    return full


# version 14 (final)
# speedup vs baseline: 1.1901x; 1.0165x over previous
import contextlib
import numpy as np
import ml_dtypes
import concourse.bass as bass
import concourse.mybir as mybir
from concourse.bass_utils import run_bass_kernel_spmd

F32 = mybir.dt.float32
BF16 = mybir.dt.bfloat16
AF = mybir.ActivationFunctionType
ALU = mybir.AluOpType
AX = mybir.AxisListType.X
BF = ml_dtypes.bfloat16

T, D, H, HD = 8192, 4096, 4, 1024
C = 32
NG1, NG2 = 16, 8
GK = 4
NCORE = 8
EPS = 1e-5
Q0, Q1 = 611, 8191
NVALID = Q1 - Q0
SCALING = 2.0


def colperm(c):
    return (c % 8) * 4 + (c // 8)


def invperm(cc):
    return 8 * (cc % 4) + cc // 4


class Sem:
    def __init__(self, es, nc, name):
        self.h = es.enter_context(nc.semaphore(name))
        self.n = 0


class Phase:

    def __init__(self, nc):
        self.nc = nc
        self.q = {k: [] for k in "tvsgy"}

    def op(self, k, method, *args, wait=(), inc=None, incn=1, **kwargs):
        tgt = None
        if inc is not None:
            inc.n += incn
            tgt = inc.n
        waits = [(s.h, v) for (s, v) in wait]

        def f(eng, method=method, args=args, kwargs=kwargs, waits=waits,
              inc=inc, incn=incn):
            for h, v in waits:
                eng.wait_ge(h, v)
            ins = getattr(eng, method)(*args, **kwargs)
            if inc is not None:
                ins.then_inc(inc.h, incn)
        self.q[k].append(f)
        return tgt

    def wait(self, k, sem, val):
        self.q[k].append(lambda eng, h=sem.h, v=val: eng.wait_ge(h, v))

    def run(self):
        with self.nc.Block() as block:
            for key, starter in (("g", block.gpsimd), ("y", block.sync),
                                 ("t", block.tensor), ("s", block.scalar),
                                 ("v", block.vector)):
                fns = self.q[key]
                if fns:
                    starter(lambda eng, fns=fns: [f(eng) for f in fns])

    def run_flat(self, engmap):
        for key in "gytsv":
            eng = engmap[key]
            for f in self.q[key]:
                f(eng)


def build_program():
    nc = bass.Bass("TRN2", target_bir_lowering=False, num_devices=NCORE)
    AP = bass.AP

    x = nc.dram_tensor("x", [T, D], F32, kind="ExternalInput")
    xt = nc.dram_tensor("xt", [D, T], F32, kind="ExternalInput")
    qsrc = nc.dram_tensor("qsrc", [2, 128, C, 512], F32, kind="ExternalInput")
    xr = nc.dram_tensor("xr", [2, 512, D], F32, kind="ExternalInput")
    mq = nc.dram_tensor("mq", [128, 8], BF16, kind="ExternalInput")
    trim = nc.dram_tensor("trim", [128, GK * 512], BF16, kind="ExternalInput")
    wt = nc.dram_tensor("wt", [D, D], F32, kind="ExternalInput")
    ats = nc.dram_tensor("ats", [128, C, 128], F32, kind="ExternalInput")
    bm = nc.dram_tensor("bm", [128, D], F32, kind="ExternalInput")
    rwts = nc.dram_tensor("rwts", [128, C, 8], F32, kind="ExternalInput")
    rbs = nc.dram_tensor("rbs", [1, 8], F32, kind="ExternalInput")
    biasr = nc.dram_tensor("biasr", [1, D], F32, kind="ExternalInput")
    bvq = nc.dram_tensor("bvq", [4, HD], F32, kind="ExternalInput")
    bvk = nc.dram_tensor("bvk", [4, HD], F32, kind="ExternalInput")
    bvv = nc.dram_tensor("bvv", [4, HD], F32, kind="ExternalInput")
    identb = nc.dram_tensor("identb", [128, 128], BF16, kind="ExternalInput")
    identf = nc.dram_tensor("identf", [128, 128], F32, kind="ExternalInput")
    ones1f = nc.dram_tensor("ones1f", [1, 128], F32, kind="ExternalInput")
    ones1b = nc.dram_tensor("ones1b", [1, 128], BF16, kind="ExternalInput")
    rep8 = nc.dram_tensor("rep8", [8, 128], F32, kind="ExternalInput")
    epsc = nc.dram_tensor("epsc", [128, 1], F32, kind="ExternalInput")

    out = nc.dram_tensor("out", [1024, D], F32, kind="ExternalOutput")

    y1sp = nc.dram_tensor("y1sp", [8, 128, D], F32, kind="Internal")
    y1td = nc.dram_tensor("y1td", [C, 128, 1024], BF16, kind="Internal")
    spd = nc.dram_tensor("spd", [8, 128, D], F32, kind="Internal")
    qsb_d = nc.dram_tensor("qsb_d", [1, D], F32, kind="Internal")
    qsa_d = nc.dram_tensor("qsa_d", [1, D], F32, kind="Internal")
    vvrow_d = nc.dram_tensor("vvrow_d", [C, 128], F32, kind="Internal")
    ccin = nc.dram_tensor("ccin", [1, D], F32, kind="Internal")
    ccout = nc.dram_tensor("ccout", [1, D], F32, kind="Internal")

    ES = contextlib.ExitStack()
    with ES:
        sbp = lambda name, shape, dt: ES.enter_context(nc.sbuf_tensor(name, shape, dt))
        identb_s = sbp("identb_s", [128, 128], BF16)
        identf_s = sbp("identf_s", [128, 128], F32)
        ones1f_s = sbp("ones1f_s", [1, 128], F32)
        ones1b_s = sbp("ones1b_s", [1, 128], BF16)
        rep8_s = sbp("rep8_s", [8, 128], F32)
        mq_s = sbp("mq_s", [128, 8], BF16)
        qkv32 = sbp("qkv32", [128, C], F32)
        stat1 = sbp("stat1", [128, 32], F32)
        stat2 = sbp("stat2", [128, 32], F32)
        svar = sbp("svar", [128, 2], F32)
        rout128 = sbp("rout128", [128, 1], F32)
        rbs_s = sbp("rbs_s", [1, 8], F32)
        epsc_s = sbp("epsc_s", [128, 1], F32)
        tri_s = sbp("tri_s", [128, GK * 512], BF16)

        sK = Sem(ES, nc, "sK")
        sV = Sem(ES, nc, "sVd")
        sG = Sem(ES, nc, "sG")
        sY = Sem(ES, nc, "sY")
        sT = Sem(ES, nc, "sT")
        sS = Sem(ES, nc, "sS")
        sE = Sem(ES, nc, "sE")
        sCC = Sem(ES, nc, "sCC")
        sX = Sem(ES, nc, "sX")
        sSp = Sem(ES, nc, "sSp")
        sTs = Sem(ES, nc, "sTs")
        sQt = Sem(ES, nc, "sQt")
        sW0 = Sem(ES, nc, "sW0")
        sW1 = Sem(ES, nc, "sW1")
        sSd = Sem(ES, nc, "sSd")
        sR0 = Sem(ES, nc, "sR0")
        sR1 = Sem(ES, nc, "sR1")
        sO = Sem(ES, nc, "sO")
        sBar = Sem(ES, nc, "sBar")
        sG2 = Sem(ES, nc, "sG2")
        sR2 = Sem(ES, nc, "sR2")
        sQ0 = Sem(ES, nc, "sQ0")
        sQ1 = Sem(ES, nc, "sQ1")
        sQ2 = Sem(ES, nc, "sQ2")

        def vop(ph, method, *args, wait=(), **kw):
            v = ph.op("v", method, *args, wait=wait, inc=sE, **kw)
            ph.wait("v", sE, v)
            return v

        def tpose(ph, out_ap, in_ap, id_ap, wait=()):
            v = ph.op("t", "transpose", out_ap, in_ap, id_ap, wait=wait, inc=sT)
            ph.wait("t", sT, v)
            return v

        ph = Phase(nc)
        for dst, src in ((identb_s, identb), (identf_s, identf), (ones1f_s, ones1f),
                         (ones1b_s, ones1b), (rep8_s, rep8), (mq_s, mq), (rbs_s, rbs),
                         (epsc_s, epsc), (tri_s, trim)):
            ph.op("y", "dma_start", dst[:, :], src.ap(), inc=sY, incn=16)
        ph.wait("y", sY, sY.n)
        ph.run()

        ph = Phase(nc)
        with contextlib.ExitStack() as E1:
            sb1 = lambda name, shape, dt: E1.enter_context(nc.sbuf_tensor(name, shape, dt))
            bv = sb1("bv", [4, 3 * HD], F32)
            hard = sb1("hard", [4, 3 * HD], F32)
            alpha = sb1("alpha", [4, 3], F32)
            tscr = sb1("tscr", [4, HD], F32)
            qkvhm = sb1("qkvhm", [4, HD], F32)
            vv32 = sb1("vv32", [128, C], F32)
            vvT32 = sb1("vvT32", [C, 128], F32)
            ps_b = E1.enter_context(nc.psum_tensor("ps_b", [128, C], F32))
            ps_c = E1.enter_context(nc.psum_tensor("ps_c", [C, 128], F32))

            for i, src in enumerate((bvq, bvk, bvv)):
                ph.op("y", "dma_start", bv[:, i * HD:(i + 1) * HD], src.ap(),
                      inc=sY, incn=16)
            wld = sY.n
            for i in range(3):
                sl = bv[:, i * HD:(i + 1) * HD]
                hsl = hard[:, i * HD:(i + 1) * HD]
                vop(ph, "reduce_sum", alpha[:, i:i + 1], sl, AX,
                    apply_absolute_value=True, wait=[(sY, wld)])
                vop(ph, "tensor_scalar", alpha[:, i:i + 1], alpha[:, i:i + 1],
                    1.0 / HD, None, ALU.mult)
                wsg = ph.op("s", "activation", hsl, sl, AF.Sign,
                            wait=[(sY, wld)], inc=sS)
                vop(ph, "tensor_tensor", tscr[:, :], hsl, hsl, ALU.mult,
                    wait=[(sS, wsg)])
                vop(ph, "tensor_scalar", tscr[:, :], tscr[:, :], -1.0, 1.0,
                    ALU.mult, ALU.add)
                vop(ph, "tensor_tensor", hsl, hsl, tscr[:, :], ALU.add)
                vop(ph, "tensor_scalar", hsl, hsl, alpha[:, i:i + 1], None, ALU.mult)
            wqk = vop(ph, "tensor_tensor", qkvhm[:, :], hard[:, 0:HD],
                      hard[:, HD:2 * HD], ALU.mult)
            for j in range(8):
                w1 = tpose(ph, ps_b[:, 4 * j:4 * j + 4],
                           qkvhm[:, j * 128:(j + 1) * 128], identf_s[0:4, 0:4],
                           wait=[(sE, wqk)] if j == 0 else ())
            wcp1 = ph.op("v", "tensor_copy", qkv32[:, :], ps_b[:, :],
                         wait=[(sT, w1)], inc=sE)
            for j in range(8):
                w2 = tpose(ph, ps_b[:, 4 * j:4 * j + 4],
                           hard[:, 2 * HD + j * 128:2 * HD + (j + 1) * 128],
                           identf_s[0:4, 0:4],
                           wait=[(sE, wcp1)] if j == 0 else ())
            wcp2 = ph.op("v", "tensor_copy", vv32[:, :], ps_b[:, :],
                         wait=[(sT, w2)], inc=sE)
            w3 = tpose(ph, ps_c[:, :], vv32[:, :], identf_s[:, :],
                       wait=[(sE, wcp2)])
            wvt = ph.op("v", "tensor_copy", vvT32[:, :], ps_c[:, :],
                        wait=[(sT, w3)], inc=sE)
            ph.op("y", "dma_start", vvrow_d.ap(), vvT32[:, :],
                  wait=[(sE, wvt)], inc=sY, incn=16)
            ph.wait("y", sY, sY.n)
            ph.run()

        with contextlib.ExitStack() as EA:
            acc = EA.enter_context(nc.sbuf_tensor("acc", [128, 4 * D], F32))

            def sweep(sweep_i, NG, diag, body):
                ph = Phase(nc)
                with contextlib.ExitStack() as E2:
                    sb2 = lambda n, sh, dt: E2.enter_context(
                        nc.sbuf_tensor(f"{n}_{sweep_i}_{body}", sh, dt))
                    Qs = sb2("Qs", [128, C * 512], BF16)
                    wqs = None
                    with nc.sbuf_tensor(f"Qraw_{sweep_i}_{body}", [128, C * 512], F32) as Qraw:
                        wqr = ph.op("g", "dma_start",
                                    AP(Qraw, 0, [[C * 512, 128], [1, C * 512]]),
                                    AP(qsrc, sweep_i * 128 * C * 512,
                                       [[C * 512, 128], [1, C * 512]]),
                                    inc=sG, incn=16)
                        for c in range(C):
                            wqs = ph.op("v", "tensor_scalar",
                                        Qs[:, c * 512:(c + 1) * 512],
                                        Qraw[:, c * 512:(c + 1) * 512],
                                        qkv32[:, colperm(c):colperm(c) + 1], None,
                                        ALU.mult,
                                        wait=[(sG, wqr)] if c == 0 else (), inc=sE)
                    Kg = sb2("Kg", [128, C * 512], BF16)
                    Vg = sb2("Vg", [128, GK * D], BF16)
                    Pt = sb2("Pt", [128, 16 * 512], BF16)
                    ps_sc = [E2.enter_context(
                        nc.psum_tensor(f"ps_sc{sweep_i}_{body}_{i}", [128, 512], F32))
                             for i in range(2)]
                    ps_o = [E2.enter_context(
                        nc.psum_tensor(f"ps_o{sweep_i}_{body}_{i}", [128, 512], F32))
                            for i in range(4)]

                    wmz = ph.op("v", "memset", acc[:, :], 0.0, inc=sE)
                    ph.wait("v", sE, wmz)

                    kdone, vdone = {}, {}
                    sigval, mm1val, maskval, chval, addval = {}, {}, {}, {}, {}
                    ph1done, ph2done = {}, {}

                    def emit_dma(g):
                        w = [(sE, wqs)] if g == 0 else [(sT, ph1done[g - 1])]
                        for cs in range(8):
                            ph.op("g", "dma_start",
                                  AP(Kg, cs * 4 * 512,
                                     [[C * 512, 128], [512, 4], [1, 512]]),
                                  AP(xt, (cs * 4 * 128) * T + 512 * g,
                                     [[T, 128], [128 * T, 4], [1, 512]]),
                                  wait=w if cs == 0 else (), inc=sK, incn=16)
                        kdone[g] = sK.n
                        wv = [(sE, wqs)] if g == 0 else [(sT, ph2done[g - 1])]
                        ph.op("g", "dma_start",
                              AP(Vg, 0, [[GK * D, 128], [D, GK], [1, D]]),
                              AP(x, 512 * g * D, [[D, 128], [128 * D, GK], [1, D]]),
                              wait=wv, inc=sV, incn=16)
                        vdone[g] = sV.n

                    emit_dma(0)
                    ci_glob = 0
                    for g in range(NG):
                        for i in range(16):
                            j, h = i // 4, i % 4
                            gi = g * 16 + i
                            bank = ps_sc[gi % 2]
                            for cc in range(8):
                                c = 8 * h + cc
                                w = []
                                if cc == 0:
                                    if i == 0:
                                        w.append((sK, kdone[g]))
                                        if g == 0:
                                            w.append((sE, wqs))
                                    if gi >= 2:
                                        w.append((sS, sigval[gi - 2]))
                                ph.op("t", "matmul", bank[:, :],
                                      Kg[:, c * 512 + j * 128:c * 512 + j * 128 + 128],
                                      Qs[:, c * 512:(c + 1) * 512],
                                      start=(cc == 0), stop=(cc == 7),
                                      wait=w, inc=sT if cc == 7 else None)
                            mm1val[gi] = sT.n
                            w = [(sT, mm1val[gi])]
                            if i == 0 and g > 0:
                                w.append((sT, ph2done[g - 1]))
                            sigval[gi] = ph.op("s", "activation",
                                               Pt[:, i * 512:(i + 1) * 512],
                                               bank[:, :], AF.Sigmoid,
                                               scale=0.125, wait=w, inc=sS)
                            if g == diag:
                                maskval[(g, i)] = ph.op(
                                    "v", "tensor_tensor", Pt[:, i * 512:(i + 1) * 512],
                                    Pt[:, i * 512:(i + 1) * 512],
                                    tri_s[:, j * 512:(j + 1) * 512], ALU.mult,
                                    wait=[(sS, sigval[gi])], inc=sE)
                        ph1done[g] = sT.n
                        first_chain = True
                        for h in range(H):
                            for qs in range(4):
                                for dh in range(2):
                                    bank = ps_o[ci_glob % 4]
                                    for j in range(GK):
                                        w = []
                                        if j == 0:
                                            if first_chain:
                                                w.append((sV, vdone[g]))
                                                if g == diag:
                                                    w.append((sE, maskval[(g, 15)]))
                                                else:
                                                    w.append((sS, sigval[g * 16 + 15]))
                                            if ci_glob >= 4:
                                                w.append((sE, addval[ci_glob - 4]))
                                        ph.op("t", "matmul", bank[:, :],
                                              Pt[:, (4 * j + h) * 512 + qs * 128:
                                                 (4 * j + h) * 512 + qs * 128 + 128],
                                              Vg[:, j * D + h * HD + dh * 512:
                                                 j * D + h * HD + dh * 512 + 512],
                                              start=(j == 0), stop=(j == GK - 1),
                                              wait=w, inc=sT if j == GK - 1 else None)
                                    chval[ci_glob] = sT.n
                                    first_chain = False
                                    wadd = [(sT, chval[ci_glob])]
                                    if ci_glob >= 32:
                                        wadd.append((sE, addval[ci_glob - 32]))
                                    addval[ci_glob] = ph.op(
                                        "v", "tensor_tensor",
                                        acc[:, qs * D + h * HD + dh * 512:
                                            qs * D + h * HD + dh * 512 + 512],
                                        acc[:, qs * D + h * HD + dh * 512:
                                            qs * D + h * HD + dh * 512 + 512],
                                        bank[:, :], ALU.add,
                                        wait=wadd, inc=sE)
                                    ci_glob += 1
                        ph2done[g] = sT.n
                        if g + 1 < NG:
                            emit_dma(g + 1)
                    ph.run_flat(ENG)

            def ln1(p, body):
                ph = Phase(nc)
                with contextlib.ExitStack() as E3:
                    sb3 = lambda n, sh, dt: E3.enter_context(
                        nc.sbuf_tensor(f"{n}_{p}_{body}", sh, dt))
                    vvrow = sb3("vvrow", [1, D], F32)
                    vvb = sb3("vvb", [128, D], F32)
                    xrb = sb3("xrb", [128, D], F32)
                    s1 = sb3("s1", [128, D], F32)
                    y1f = sb3("y1f", [128, D], F32)
                    y1b4 = sb3("y1b4", [128, 4 * D], BF16)
                    y1ts = [sb3(f"y1ts{i}", [128, 1024], BF16) for i in range(2)]
                    junk = sb3("junk", [128, D], BF16)
                    qtmp = sb3("qtmp", [1, 1024], F32)
                    ps_tp = [E3.enter_context(
                        nc.psum_tensor(f"ps_tp{p}_{body}_{i}", [128, 1024], BF16))
                             for i in range(2)]
                    ps_vb = [E3.enter_context(
                        nc.psum_tensor(f"ps_vb{p}_{body}_{i}", [128, 512], F32))
                             for i in range(2)]
                    ps_q = [E3.enter_context(
                        nc.psum_tensor(f"ps_q{p}_{body}_{i}", [1, 512], F32))
                            for i in range(2)]

                    wvr0 = ph.op("y", "dma_start", vvrow[:, :],
                                 bass.AP(vvrow_d, 0, [[C * 128, 1], [1, C * 128]]),
                                 inc=sY, incn=16)
                    vbmm, vbev = {}, {}
                    for grp in range(8):
                        bank = ps_vb[grp % 2]
                        for ci in range(4):
                            cc = grp * 4 + ci
                            w = []
                            if ci == 0 and grp == 0:
                                w.append((sY, wvr0))
                            if ci == 0 and grp >= 2:
                                w.append((sE, vbev[grp - 2]))
                            vbmm[grp] = ph.op(
                                "t", "matmul", bank[:, ci * 128:(ci + 1) * 128],
                                ones1f_s[:, :], vvrow[0:1, cc * 128:(cc + 1) * 128],
                                start=True, stop=True, wait=w, inc=sT)
                            ph.wait("t", sT, vbmm[grp])
                        for ci in range(4):
                            cc = grp * 4 + ci
                            c = invperm(cc)
                            vbev[grp] = ph.op("v", "tensor_copy",
                                              vvb[:, c * 128:(c + 1) * 128],
                                              bank[:, ci * 128:(ci + 1) * 128],
                                              wait=[(sT, vbmm[grp])] if ci == 0 else (),
                                              inc=sE)
                    xrd, s1dn, y1fd, y1bd = {}, {}, {}, {}
                    spl = {}
                    for k in range(4):
                        tg = p * 4 + k
                        w = [] if k == 0 else [(sE, s1dn[k - 1])]
                        ph.op("y", "dma_start",
                              AP(xrb, 0, [[D, 128], [1, D]]),
                              AP(xr, (p * 512 + k * 128) * D, [[D, 128], [1, D]]),
                              wait=w, inc=sX, incn=16)
                        xrd[k] = sX.n
                        w = [(sE, vbev[7])] if k == 0 else [(sE, y1fd[k - 1])]
                        wtt1 = ph.op("v", "tensor_tensor", s1[:, :],
                                     acc[:, k * D:(k + 1) * D], vvb[:, :], ALU.mult,
                                     wait=w, inc=sE)
                        ph.wait("v", sE, wtt1)
                        s1dn[k] = ph.op("v", "scalar_tensor_tensor", s1[:, :],
                                        s1[:, :], 1.0, xrb[:, :], ALU.mult, ALU.add,
                                        accum_out=stat1[:, tg * 4:tg * 4 + 1],
                                        wait=[(sX, xrd[k])], inc=sE)
                        wsqw = [(sE, s1dn[k])]
                        if k >= 1:
                            wsqw.append((sS, y1bd[k - 1]))
                        wsq = ph.op("s", "activation", junk[:, :], s1[:, :], AF.Square,
                                    accum_out=stat1[:, tg * 4 + 1:tg * 4 + 2],
                                    wait=wsqw, inc=sS)
                        vop(ph, "tensor_scalar", stat1[:, tg * 4 + 2:tg * 4 + 3],
                            stat1[:, tg * 4:tg * 4 + 1], -1.0 / D, None, ALU.mult,
                            wait=[(sS, wsq)])
                        vop(ph, "tensor_scalar", svar[:, 0:1],
                            stat1[:, tg * 4 + 1:tg * 4 + 2], 1.0 / D, None, ALU.mult)
                        vop(ph, "tensor_tensor", svar[:, 1:2],
                            stat1[:, tg * 4 + 2:tg * 4 + 3],
                            stat1[:, tg * 4 + 2:tg * 4 + 3], ALU.mult)
                        wvr = vop(ph, "tensor_tensor", svar[:, 0:1], svar[:, 0:1],
                                  svar[:, 1:2], ALU.subtract)
                        wsd = ph.op("s", "activation", svar[:, 1:2], svar[:, 0:1],
                                    AF.Sqrt, bias=epsc_s[:, :], wait=[(sE, wvr)], inc=sS)
                        vop(ph, "reciprocal", stat1[:, tg * 4 + 3:tg * 4 + 4],
                            svar[:, 1:2], wait=[(sS, wsd)])
                        w = []
                        if k >= 1:
                            w.append((sSp, spl[k - 1]))
                            w.append((sS, y1bd[k - 1]))
                        y1fd[k] = ph.op("v", "tensor_scalar", y1f[:, :], s1[:, :],
                                        stat1[:, tg * 4 + 2:tg * 4 + 3],
                                        stat1[:, tg * 4 + 3:tg * 4 + 4],
                                        ALU.add, ALU.mult, wait=w, inc=sE)
                        ph.op("y", "dma_start",
                              AP(y1sp, tg * 128 * D, [[D, 128], [1, D]]),
                              AP(y1f, 0, [[D, 128], [1, D]]),
                              wait=[(sE, y1fd[k])], inc=sSp, incn=16)
                        spl[k] = sSp.n
                        y1bd[k] = ph.op("s", "copy", y1b4[:, k * D:(k + 1) * D],
                                        y1f[:, :],
                                        wait=[(sE, y1fd[k])], inc=sS)
                    tpmm, tpev, tsd = {}, {}, {}
                    for cg in range(16):
                        bank = ps_tp[cg % 2]
                        for ci in range(2):
                            c = cg * 2 + ci
                            for tk in range(4):
                                w = []
                                if ci == 0 and tk == 0:
                                    if cg == 0:
                                        w.append((sS, y1bd[3]))
                                    if cg >= 2:
                                        w.append((sE, tpev[cg - 2]))
                                tpmm[cg] = tpose(
                                    ph,
                                    bank[:, (ci * 4 + tk) * 128:(ci * 4 + tk + 1) * 128],
                                    y1b4[:, tk * D + c * 128:tk * D + (c + 1) * 128],
                                    identb_s[:, :], wait=w)
                        w = [(sT, tpmm[cg])]
                        if cg >= 1:
                            w.append((sTs, tsd[cg - 1]))
                        tpev[cg] = ph.op("v", "tensor_copy", y1ts[cg % 2][:, :],
                                         bank[:, :], wait=w, inc=sE)
                        ph.op("y", "dma_start",
                              AP(y1td, (cg * 2) * 128 * 1024 + p * 512,
                                 [[1024, 128], [128 * 1024, 2], [1, 512]]),
                              AP(y1ts[cg % 2], 0, [[1024, 128], [512, 2], [1, 512]]),
                              wait=[(sE, tpev[cg])], inc=sTs, incn=16)
                        tsd[cg] = sTs.n
                    qmm, qev, qdm = {}, {}, {}
                    for fc in range(8):
                        bank = ps_q[fc % 2]
                        for tk in range(4):
                            w = []
                            if tk == 0 and fc >= 2:
                                w.append((sE, qev[fc - 2]))
                            ph.op("t", "matmul", bank[:, :],
                                  mq_s[:, p * 4 + tk:p * 4 + tk + 1],
                                  y1b4[:, tk * D + fc * 512:tk * D + (fc + 1) * 512],
                                  start=(tk == 0), stop=(tk == 3),
                                  wait=w, inc=sT if tk == 3 else None)
                        qmm[fc] = sT.n
                        w = [(sT, qmm[fc])]
                        if fc >= 1:
                            w.append((sQt, qdm[fc - 1]))
                        qev[fc] = ph.op("v", "tensor_copy",
                                        qtmp[0:1, (fc % 2) * 512:(fc % 2) * 512 + 512],
                                        bank[0:1, :], wait=w, inc=sE)
                        qd = qsb_d if p == 0 else qsa_d
                        ph.op("y", "dma_start",
                              qd[0:1, fc * 512:(fc + 1) * 512],
                              qtmp[0:1, (fc % 2) * 512:(fc % 2) * 512 + 512],
                              wait=[(sE, qev[fc])], inc=sQt, incn=16)
                        qdm[fc] = sQt.n
                    ph.wait("y", sY, sY.n)
                    ph.wait("y", sX, sX.n)
                    ph.wait("y", sSp, sSp.n)
                    ph.wait("y", sTs, sTs.n)
                    ph.wait("y", sQt, sQt.n)
                    ph.run()

            engs = [nc.tensor, nc.vector, nc.scalar, nc.gpsimd, nc.sync]
            ENG = {"t": nc.tensor, "v": nc.vector, "s": nc.scalar,
                   "g": nc.gpsimd, "y": nc.sync}
            body_sems = [sK, sV, sG, sY, sX, sSp, sTs, sQt, sT, sS, sE]

            carry = {b: {} for b in range(8)}

            def swept(sweep_i):
                NGmax = NG1 if sweep_i == 0 else NG2
                PAD = {id(sT): 48 * NGmax, id(sS): 16 * NGmax,
                       id(sE): 1 + 32 + 16 + 32 * NGmax}
                dma_sems = [sK, sV, sG]
                pad_sems = [sT, sS, sE]
                snap = [(s, s.n) for s in body_sems]
                sbar0 = sBar.n
                idx = [e.partition_id() for e in engs]
                for body in nc.Switch(engines=engs, index=idx, n=8):
                    for s, v in snap:
                        s.n = v
                    for s in dma_sems:
                        if id(s) in carry[body]:
                            s.n = carry[body][id(s)]
                    sBar.n = sbar0
                    if sweep_i == 0:
                        sweep(0, 16 - body, 15 - body, body)
                    else:
                        sweep(1, body + 1, body, body)
                    for s, _ in snap:
                        nc.gpsimd.wait_ge(s.h, s.n)
                    for s, v0 in snap:
                        if id(s) in PAD:
                            assert s.n - v0 <= PAD[id(s)], (s.h.name, s.n - v0)
                            pad = v0 + PAD[id(s)] - s.n
                            if pad:
                                nc.gpsimd.sem_inc(s.h, pad)
                    for s in dma_sems:
                        carry[body][id(s)] = s.n
                    tgt = sBar.n + 5
                    for e in engs:
                        e.nop().then_inc(sBar.h, 1)
                    for e in engs:
                        e.wait_ge(sBar.h, tgt)
                    sBar.n = tgt
                for s, v0 in snap:
                    s.n = v0 + PAD[id(s)] if id(s) in PAD else v0

            swept(0)
            ln1(0, 0)
            swept(1)
            ln1(1, 0)

        with contextlib.ExitStack() as EF:
            sbf = lambda n, sh, dt: EF.enter_context(nc.sbuf_tensor(n, sh, dt))
            y1T = sbf("y1T", [128, C * 1024], BF16)
            Abf = sbf("Abf", [128, C * 128], BF16)
            Bmbf = sbf("Bmbf", [128, D], BF16)
            rwtsb = sbf("rwtsb", [128, C * 8], BF16)
            bb16 = sbf("bb16", [1, D], BF16)
            tTs = sbf("tTs", [128, 1024], BF16)

            ph = Phase(nc)
            with contextlib.ExitStack() as E6:
                sb6 = lambda n, sh, dt: E6.enter_context(nc.sbuf_tensor(n, sh, dt))
                qrow = sb6("qrow", [1, D], F32)
                qrow2 = sb6("qrow2", [1, D], F32)
                qmf = sb6("qmf", [128, C], F32)
                qmfb = sb6("qmfb", [128, C], BF16)
                lg = sb6("lg", [1, 8], F32)
                mx = sb6("mx", [1, 1], F32)
                ex = sb6("ex", [1, 8], F32)
                sm = sb6("sm", [1, 1], F32)
                rt = sb6("rt", [1, 8], F32)
                r8 = sb6("r8", [8, 1], F32)
                ps_r = E6.enter_context(nc.psum_tensor("ps_r", [1, 8], F32))
                ps_q2 = E6.enter_context(nc.psum_tensor("ps_q2", [128, C], F32))
                ps_r8 = E6.enter_context(nc.psum_tensor("ps_r8", [8, 1], F32))
                ps_r128 = E6.enter_context(nc.psum_tensor("ps_r128", [128, 1], F32))

                for cs in range(8):
                    ph.op("g", "dma_start",
                          AP(y1T, cs * 4 * 1024, [[C * 1024, 128], [1024, 4], [1, 1024]]),
                          AP(y1td, cs * 4 * 128 * 1024,
                             [[1024, 128], [128 * 1024, 4], [1, 1024]]),
                          inc=sG2, incn=16)
                ph.op("g", "dma_start", Abf[:, :],
                      AP(ats, 0, [[C * 128, 128], [1, C * 128]]), inc=sG2, incn=16)
                ph.op("g", "dma_start", Bmbf[:, :], bm.ap(), inc=sG2, incn=16)
                ph.op("g", "dma_start", rwtsb[:, :],
                      AP(rwts, 0, [[C * 8, 128], [1, C * 8]]), inc=sG2, incn=16)
                ph.op("g", "dma_start", bb16[:, :], biasr.ap(), inc=sG2, incn=16)
                wffl = sG2.n

                ph.op("y", "dma_start", qrow[:, :], qsb_d.ap(), inc=sY, incn=16)
                ph.op("y", "dma_start", qrow2[:, :], qsa_d.ap(), inc=sY, incn=16)
                wq2 = sY.n
                wqa = ph.op("v", "tensor_tensor", qrow[:, :], qrow[:, :], qrow2[:, :],
                            ALU.add, wait=[(sY, wq2)], inc=sE)
                wcci = ph.op("y", "dma_start", ccin.ap(), qrow[:, :],
                             wait=[(sE, wqa)], inc=sY, incn=16)
                ph.op("g", "collective_compute", "AllReduce", ALU.add,
                      replica_groups=[list(range(NCORE))],
                      ins=[ccin.ap().opt()], outs=[ccout.ap().opt()],
                      wait=[(sY, wcci)], inc=sCC, incn=1)
                wccr = ph.op("y", "dma_start", qrow[:, :], ccout.ap(),
                             wait=[(sCC, 1)], inc=sY, incn=16)
                wqm = ph.op("v", "tensor_scalar", qrow[:, :], qrow[:, :],
                            1.0 / NVALID, None, ALU.mult,
                            wait=[(sY, wccr)], inc=sE)
                for j in range(C):
                    wq2t = tpose(ph, ps_q2[:, j:j + 1],
                                 qrow[0:1, j * 128:(j + 1) * 128], identf_s[0:1, 0:1],
                                 wait=[(sE, wqm)] if j == 0 else ())
                vop(ph, "tensor_copy", qmf[:, :], ps_q2[:, :], wait=[(sT, wq2t)])
                wqb = vop(ph, "tensor_copy", qmfb[:, :], qmf[:, :])
                for c in range(C):
                    w = []
                    if c == 0:
                        w = [(sE, wqb), (sG2, wffl)]
                    ph.op("t", "matmul", ps_r[:, :], qmfb[:, c:c + 1],
                          rwtsb[:, c * 8:(c + 1) * 8],
                          start=(c == 0), stop=(c == C - 1),
                          wait=w, inc=sT if c == C - 1 else None)
                wlg = sT.n
                vop(ph, "scalar_tensor_tensor", lg[:, :], ps_r[:, :], 1.0,
                    rbs_s[:, :], ALU.mult, ALU.add, wait=[(sT, wlg)])
                vop(ph, "reduce_max", mx[:, :], lg[:, :], AX)
                wmx = vop(ph, "tensor_scalar", mx[:, :], mx[:, :], -1.0, None,
                          ALU.mult)
                wex = ph.op("s", "activation", ex[:, :], lg[:, :], AF.Exp,
                            bias=mx[:, :], wait=[(sE, wmx)], inc=sS)
                vop(ph, "reduce_sum", sm[:, :], ex[:, :], AX, wait=[(sS, wex)])
                vop(ph, "reciprocal", sm[:, :], sm[:, :])
                wrt = vop(ph, "tensor_scalar", rt[:, :], ex[:, :], sm[:, :],
                          SCALING, ALU.mult, ALU.mult)
                wr8 = tpose(ph, ps_r8[:, :], rt[0:1, :],
                            identf_s[0:1, 0:1], wait=[(sE, wrt)])
                wr8e = ph.op("v", "tensor_copy", r8[:, :], ps_r8[:, :],
                             wait=[(sT, wr8)], inc=sE)
                wrmm = ph.op("t", "matmul", ps_r128[:, :], rep8_s[:, :], r8[:, :],
                             start=True, stop=True, wait=[(sE, wr8e)], inc=sT)
                wrout = ph.op("v", "tensor_copy", rout128[:, :], ps_r128[:, :],
                              wait=[(sT, wrmm)], inc=sE)
                ph.run()

            ph = Phase(nc)
            with contextlib.ExitStack() as E7:
                ps_tt = [E7.enter_context(nc.psum_tensor(f"ps_tt{i}", [128, 512], F32))
                         for i in range(2)]
                for tb in range(2):
                    for c in range(C):
                        w = [(sG2, wffl)] if (tb == 0 and c == 0) else []
                        ph.op("t", "matmul", ps_tt[tb][:, :],
                              Abf[:, c * 128:(c + 1) * 128],
                              y1T[:, c * 1024 + tb * 512:c * 1024 + tb * 512 + 512],
                              start=(c == 0), stop=(c == C - 1),
                              wait=w, inc=sT if c == C - 1 else None)
                    wtt = sT.n
                    ph.op("v", "tensor_scalar", tTs[:, tb * 512:(tb + 1) * 512],
                          ps_tt[tb][:, :], rout128[:, :], None, ALU.mult,
                          wait=[(sT, wtt), (sE, wrout)], inc=sE)
                wtts = sE.n
                ph.run()

            ph = Phase(nc)
            with contextlib.ExitStack() as E8:
                sb8 = lambda n, sh, dt: E8.enter_context(nc.sbuf_tensor(n, sh, dt))
                Wo = [sb8(f"Wo{i}", [128, C * 512], BF16) for i in range(2)]
                stmp = [sb8(f"stmp{i}", [128, 512], F32) for i in range(4)]
                ps_bs = [E8.enter_context(nc.psum_tensor(f"ps_bs{i}", [128, 512], F32))
                         for i in range(8)]

                wW, bsd, stev, stw = {}, {}, {}, {}

                def emit_w(o):
                    sWp = sW0 if o % 2 == 0 else sW1
                    w = []
                    if o >= 2:
                        w = [(sT, bsd[(o - 2, 7)])]
                    for dg in range(8):
                        ph.op("g", "dma_start",
                              AP(Wo[o % 2], dg * 4 * 512,
                                 [[C * 512, 128], [512, 4], [1, 512]]),
                              AP(wt, (dg * 4 * 128) * D + o * 512,
                                 [[D, 128], [128 * D, 4], [1, 512]]),
                              wait=w if dg == 0 else (), inc=sWp, incn=16)
                    wW[o] = (sWp, sWp.n)

                emit_w(0)
                emit_w(1)
                for o in range(8):
                    for half in range(2):
                        for dg in range(8):
                            for dc in range(4):
                                cch = dg * 4 + dc
                                for tt in range(4):
                                    tglob = half * 4 + tt
                                    w = []
                                    if dg == 0 and dc == 0 and tt == 0:
                                        w.append(wW[o])
                                        if half == 0 and o == 0:
                                            w.append((sE, wtts))
                                        if o >= 1:
                                            w.append((sE, stev[(o - 1, half * 4 + 3)]))
                                    ph.op("t", "matmul", ps_bs[tglob][:, :],
                                          y1T[:, cch * 1024 + tglob * 128:
                                              cch * 1024 + tglob * 128 + 128],
                                          Wo[o % 2][:, cch * 512:(cch + 1) * 512],
                                          start=(dg == 0 and dc == 0), stop=False,
                                          wait=w)
                        for tt in range(4):
                            tglob = half * 4 + tt
                            ph.op("t", "matmul", ps_bs[tglob][:, :],
                                  tTs[:, tglob * 128:(tglob + 1) * 128],
                                  Bmbf[:, o * 512:(o + 1) * 512],
                                  start=False, stop=False)
                            ph.op("t", "matmul", ps_bs[tglob][:, :],
                                  ones1b_s[:, :], bb16[0:1, o * 512:(o + 1) * 512],
                                  start=False, stop=True, inc=sT)
                            bsd[(o, tglob)] = sT.n
                        for tt in range(4):
                            tglob = half * 4 + tt
                            w = [(sT, bsd[(o, tglob)])]
                            ko = (o * 8 + tglob)
                            if ko >= 1:
                                w.append((sSd, stw[ko - 1]))
                            stev[(o, tglob)] = ph.op("v", "tensor_copy",
                                                     stmp[tglob % 4][:, :],
                                                     ps_bs[tglob][:, :], wait=w, inc=sE)
                            ph.op("y", "dma_start",
                                  AP(spd, tglob * 128 * D + o * 512,
                                     [[D, 128], [1, 512]]),
                                  AP(stmp[tglob % 4], 0, [[512, 128], [1, 512]]),
                                  wait=[(sE, stev[(o, tglob)])], inc=sSd, incn=16)
                            stw[ko] = sSd.n
                    if o + 2 < 8:
                        emit_w(o + 2)
                ph.wait("y", sSd, sSd.n)
                ph.run()

        ph = Phase(nc)
        with contextlib.ExitStack() as E9:
            sb9 = lambda n, sh, dt: E9.enter_context(nc.sbuf_tensor(n, sh, dt))
            spb = [sb9(f"spb{i}", [128, D], F32) for i in range(3)]
            y1rb = [sb9(f"y1rb{i}", [128, D], F32) for i in range(3)]
            s2 = sb9("s2", [128, D], F32)
            o2 = [sb9(f"o2{i}", [128, D], F32) for i in range(2)]
            junk2 = sb9("junk2", [128, D], BF16)

            spr, y1r, s2dn, odn, outd = {}, {}, {}, {}, {}
            wsd_prev = [0]
            for tg in range(8):
                sRp = [sR0, sR1, sR2][tg % 3]
                sQp = [sQ0, sQ1, sQ2][tg % 3]
                w = []
                if tg >= 3:
                    w = [(sE, s2dn[tg - 3])]
                ph.op("y", "dma_start", AP(spb[tg % 3], 0, [[D, 128], [1, D]]),
                      AP(spd, tg * 128 * D, [[D, 128], [1, D]]),
                      wait=w, inc=sRp, incn=16)
                ph.op("g", "dma_start", AP(y1rb[tg % 3], 0, [[D, 128], [1, D]]),
                      AP(y1sp, tg * 128 * D, [[D, 128], [1, D]]),
                      wait=list(w), inc=sQp, incn=16)
                y1r[tg] = sQp.n
                ws2 = [(sRp, sRp.n), (sQp, y1r[tg])]
                if tg >= 1:
                    ws2.append((sE, odn[tg - 1]))
                s2dn[tg] = ph.op("v", "scalar_tensor_tensor", s2[:, :],
                                 spb[tg % 3][:, :], 1.0, y1rb[tg % 3][:, :],
                                 ALU.mult, ALU.add,
                                 accum_out=stat2[:, tg * 4:tg * 4 + 1],
                                 wait=ws2, inc=sE)
                wsqw = [(sE, s2dn[tg])]
                if tg >= 1:
                    wsqw.append((sS, wsd_prev[0]))
                wsq = ph.op("s", "activation", junk2[:, :], s2[:, :], AF.Square,
                            accum_out=stat2[:, tg * 4 + 1:tg * 4 + 2],
                            wait=wsqw, inc=sS)
                vop(ph, "tensor_scalar", stat2[:, tg * 4 + 2:tg * 4 + 3],
                    stat2[:, tg * 4:tg * 4 + 1], -1.0 / D, None, ALU.mult,
                    wait=[(sS, wsq)])
                vop(ph, "tensor_scalar", svar[:, 0:1],
                    stat2[:, tg * 4 + 1:tg * 4 + 2], 1.0 / D, None, ALU.mult)
                vop(ph, "tensor_tensor", svar[:, 1:2],
                    stat2[:, tg * 4 + 2:tg * 4 + 3],
                    stat2[:, tg * 4 + 2:tg * 4 + 3], ALU.mult)
                wvr = vop(ph, "tensor_tensor", svar[:, 0:1], svar[:, 0:1],
                          svar[:, 1:2], ALU.subtract)
                wsd = ph.op("s", "activation", svar[:, 1:2], svar[:, 0:1], AF.Sqrt,
                            bias=epsc_s[:, :], wait=[(sE, wvr)], inc=sS)
                wsd_prev[0] = wsd
                vop(ph, "reciprocal", stat2[:, tg * 4 + 3:tg * 4 + 4], svar[:, 1:2],
                    wait=[(sS, wsd)])
                w = []
                if tg >= 1:
                    w.append((sO, outd[tg - 1]))
                odn[tg] = ph.op("v", "tensor_scalar", o2[tg % 2][:, :], s2[:, :],
                                stat2[:, tg * 4 + 2:tg * 4 + 3],
                                stat2[:, tg * 4 + 3:tg * 4 + 4],
                                ALU.add, ALU.mult, wait=w, inc=sE)
                ph.op("y", "dma_start", AP(out, tg * 128 * D, [[D, 128], [1, D]]),
                      AP(o2[tg % 2], 0, [[D, 128], [1, D]]),
                      wait=[(sE, odn[tg])], inc=sO, incn=16)
                outd[tg] = sO.n
            ph.wait("y", sO, sO.n)
            ph.run()

    return nc


_NC_CACHE = [None]


def _tri():
    tri = np.zeros((128, GK, 512), np.float32)
    ii = np.arange(128)[:, None]
    qq = np.arange(512)[None, :]
    for j in range(GK):
        tri[:, j, :] = (qq >= j * 128 + ii).astype(np.float32)
    return np.ascontiguousarray(tri.reshape(128, GK * 512)).astype(BF)


def _rearr_q(rows):
    return np.ascontiguousarray(rows.T.reshape(C, 128, 512).transpose(1, 0, 2))


def kernel(L0_attn_x, L0_attn_bv_q, L0_attn_bv_k, L0_attn_bv_v,
           L0_ffn_W, L0_ffn_b, L0_ffn_A, L0_ffn_B,
           L0_ffn_router_W, L0_ffn_router_b):
    X = np.ascontiguousarray(np.asarray(L0_attn_x, np.float32).reshape(T, D))
    XT = np.ascontiguousarray(X.T)
    WT = np.ascontiguousarray(np.asarray(L0_ffn_W, np.float32).T)
    AT = np.asarray(L0_ffn_A, np.float32).reshape(128, D).T
    ATS = np.ascontiguousarray(AT.reshape(C, 128, 128).transpose(1, 0, 2))
    BM = np.ascontiguousarray(
        np.asarray(L0_ffn_B, np.float32).transpose(0, 2, 1).reshape(128, D))
    RWT = np.asarray(L0_ffn_router_W, np.float32).T
    RWTS = np.ascontiguousarray(RWT.reshape(C, 128, 8).transpose(1, 0, 2))
    rb = np.ascontiguousarray(np.asarray(L0_ffn_router_b, np.float32).reshape(1, 8))
    bias = np.ascontiguousarray(np.asarray(L0_ffn_b, np.float32).reshape(1, D))
    bvqa = np.ascontiguousarray(np.asarray(L0_attn_bv_q, np.float32))
    bvka = np.ascontiguousarray(np.asarray(L0_attn_bv_k, np.float32))
    bvva = np.ascontiguousarray(np.asarray(L0_attn_bv_v, np.float32))

    shared = dict(
        x=X, xt=XT, wt=WT, ats=ATS, bm=BM, rwts=RWTS, rbs=rb, biasr=bias,
        bvq=bvqa, bvk=bvka, bvv=bvva,
        identb=np.eye(128, dtype=BF),
        identf=np.eye(128, dtype=np.float32),
        ones1f=np.ones((1, 128), np.float32),
        ones1b=np.ones((1, 128), BF),
        rep8=np.repeat(np.eye(8, dtype=np.float32), 16, axis=1),
        epsc=np.full((128, 1), EPS, np.float32),
        trim=_tri(),
    )
    in_maps = []
    for c in range(NCORE):
        B0 = 512 * (15 - c)
        A0 = 512 * c
        q = np.stack([_rearr_q(X[B0:B0 + 512]), _rearr_q(X[A0:A0 + 512])])
        xrh = np.stack([X[B0:B0 + 512], X[A0:A0 + 512]])
        loc = np.concatenate([np.arange(B0, B0 + 512), np.arange(A0, A0 + 512)])
        mqv = ((loc >= Q0) & (loc < Q1)).astype(np.float32)
        mqs = np.ascontiguousarray(mqv.reshape(8, 128).T).astype(BF)
        m = dict(shared)
        m.update(qsrc=q, xr=np.ascontiguousarray(xrh), mq=mqs)
        in_maps.append(m)

    if _NC_CACHE[0] is None:
        _NC_CACHE[0] = build_program()
    nc = _NC_CACHE[0]
    res = run_bass_kernel_spmd(nc, in_maps, core_ids=list(range(NCORE)))

    full = np.empty((1, T, D), np.float32)
    for c in range(NCORE):
        r = res.results[c]["out"]
        B0 = 512 * (15 - c)
        A0 = 512 * c
        full[0, B0:B0 + 512] = r[0:512]
        full[0, A0:A0 + 512] = r[512:1024]
    return full
